# version 01 (final)
# baseline (speedup 1.0000x reference)
import numpy as np
from contextlib import ExitStack

import concourse.bass as bass
import concourse.mybir as mybir
from concourse.bass_utils import run_bass_kernel_spmd

F32 = mybir.dt.float32
BF16 = mybir.dt.bfloat16
U32 = mybir.dt.uint32
AF = mybir.ActivationFunctionType
ALU = mybir.AluOpType
AX = mybir.AxisListType

D = 1024
T = 4096
TOWN = 2048
NDC = 8
NB = 8
NBO = 4
NTT = 32
H_FF = 2816
NHC = 22
NE = 8
NHEADS = 8
EPS_LN = 1e-5
EPS_K = 1e-6
LAM_INIT = 0.8
QSCALE = float(1.0 / np.sqrt(128.0))
CB_Q = float(1.02 * np.sqrt(128.0) / 2.0)
CB_K = float(1.02 / (2.0 * np.sqrt(128.0)))

_CACHE = {}


class Sem:
    def __init__(self, nc, es, name):
        self.h = es.enter_context(nc.semaphore(name))
        self.n = 0

    def inc(self, instr, k=1):
        instr.then_inc(self.h, k)
        self.n += k
        return instr

    def wait(self, eng, n):
        if n > 0:
            eng.wait_ge(self.h, n)


def build_program(upto="p6", dump=(), ACCUM_ON=True, NEB=NE, P6S=5):
    dbg = set(dump)
    nc = bass.Bass("TRN2", target_bir_lowering=False, debug=False)
    es = ExitStack()

    def dram(name, shape, dt, kind):
        return nc.dram_tensor(name, shape, dt, kind=kind).ap()

    def sb(stack, name, shape, dt):
        return stack.enter_context(nc.sbuf_tensor(name, shape, dt))

    i_x = dram("x_in_T", [D, T], F32, "ExternalInput")
    i_hflag = dram("hflag", [1, 1], U32, "ExternalInput")
    iw = {n: dram(n, [D, D], F32, "ExternalInput")
          for n in ["w0q", "w0k", "w0v", "w0c", "w0fr", "w0fk", "w0fv",
                    "w1q", "w1k", "w1v", "w1o"]}
    i_vec = {n: dram(n, [128, NDC], F32, "ExternalInput")
             for n in ["mur_c", "muk_c", "bq_c", "bk_c", "bo_c", "lnw_c",
                       "lnb_c"]}
    i_bvr = dram("bv_r", [1, D], F32, "ExternalInput")
    i_lam = dram("lamv", [1, 512], F32, "ExternalInput")
    i_wgT = dram("wgT", [D, NE], F32, "ExternalInput")
    i_w1t = dram("w1t", [NE, NHC, 128, NDC * 128], F32, "ExternalInput")
    i_w2t = dram("w2t", [NE, NHC, 128, NDC * 128], F32, "ExternalInput")
    i_w3t = dram("w3t", [NE, NHC, 128, D], F32, "ExternalInput")
    i_tri = dram("c_tri", [128, 128], F32, "ExternalInput")
    i_id = dram("c_id", [128, 128], F32, "ExternalInput")
    i_ones = dram("c_ones", [128, 128], F32, "ExternalInput")
    i_init = dram("c_init", [128, 2], F32, "ExternalInput")

    s_shbm = dram("s_hbm", [128, NTT * 1024], BF16, "Internal")
    s_khbm = dram("k_hbm", [D, T], BF16, "Internal")
    s_qhbm = dram("q_hbm", [D, T], BF16, "Internal")
    s_x2hbm = dram("x2_hbm", [D, T], BF16, "Internal")
    s_athbm = dram("att_hbm", [D, TOWN], BF16, "Internal")
    s_x3hbm = dram("x3_hbm", [D, TOWN], BF16, "Internal")
    s_wrow = dram("wrow_scr", [NE, TOWN], F32, "Internal")
    s_mosc = dram("moe_scr", [D, TOWN], F32, "Internal")
    o_out = dram("out_T", [D, TOWN], F32, "ExternalOutput")

    dbg_t = {}

    def dbg_out(name, shape, dt):
        if name in dbg:
            dbg_t[name] = dram("dbg_" + name, shape, dt, "ExternalOutput")
        return dbg_t.get(name)

    d_mr = dbg_out("mr", [2, T], BF16)
    d_lam = dbg_out("lam", [128, 9], F32)
    d_S = dbg_out("S", [128, NTT * 1024], BF16)
    d_stream = dbg_out("stream", [128, NDC * T], BF16)
    d_vv = dbg_out("vv", [128, NTT * 512], BF16)
    d_att = dbg_out("att", [128, NHEADS * 4 * 512], BF16)
    d_x3 = dbg_out("x3", [128, NBO * NDC * 512], BF16)
    d_sc = dbg_out("sc", [128, 16 * NE], F32)
    d_cnt = dbg_out("cnt", [128, 16 * NE], F32)
    d_w = dbg_out("w", [NE, TOWN], F32)

    s_gd = Sem(nc, es, "s_gd")
    s_sd = Sem(nc, es, "s_sd")
    s_mm = Sem(nc, es, "s_mm")
    s_se = Sem(nc, es, "s_se")
    s_ve = Sem(nc, es, "s_ve")
    SEMS = [s_gd, s_sd, s_mm, s_se, s_ve]

    def snap():
        return [s.n for s in SEMS]

    def bar(eng, snp):
        for s, n in zip(SEMS, snp):
            s.wait(eng, n)

    c_tri = sb(es, "sb_tri", [128, 128], BF16)
    c_id = sb(es, "sb_id", [128, 128], F32)
    c_ones = sb(es, "sb_ones", [128, 128], BF16)
    c_init = sb(es, "sb_init", [128, 2], F32)
    vsb = {n: sb(es, "v_" + n, [128, NDC], F32)
           for n in ["mur_c", "muk_c", "bq_c", "bk_c", "bo_c", "lnw_c"]}
    v_lnb = sb(es, "v_lnb", [128, NDC], BF16)
    v_bqs = sb(es, "v_bqs", [128, NDC], F32)
    v_lnw02 = sb(es, "v_lnw02", [128, NDC], F32)
    v_bvr = sb(es, "v_bvr", [1, D], BF16)
    v_lamr = sb(es, "v_lamr", [1, 512], F32)
    v_lsc = sb(es, "v_lsc", [1, 8], F32)
    v_lscb = sb(es, "v_lscb", [1, 1], BF16)
    neglam = sb(es, "neglam", [128, 1], F32)
    wg_f = sb(es, "wg_f", [128, NDC * NE], F32)
    wg_hi = sb(es, "wg_hi", [128, NDC * NE], BF16)
    wg_mid = sb(es, "wg_mid", [128, NDC * NE], BF16)
    wg_lo = sb(es, "wg_lo", [128, NDC * NE], BF16)
    wg_t1 = sb(es, "wg_t1", [128, NDC * NE], F32)
    hs = sb(es, "hs", [1, 1], U32)
    c_onesf = sb(es, "c_onesf", [1, 128], F32)
    mb_rows = sb(es, "mb_rows", [1, T], BF16)
    rb_rows = sb(es, "rb_rows", [1, T], BF16)
    rowm = sb(es, "rowm", [1, 512], F32)
    rowr = sb(es, "rowr", [1, 512], F32)
    rowtmp = sb(es, "rowtmp", [1, 512], F32)
    rowtmp2 = sb(es, "rowtmp2", [1, 512], F32)
    lnx_sq = sb(es, "lnx_sq", [128, NDC * 512], BF16)
    mbt = sb(es, "mbt", [128, 512], F32)
    rbt = sb(es, "rbt", [128, 512], F32)
    streamscope = ExitStack()
    stream = sb(streamscope, "stream", [128, NDC * T], BF16)

    onescol = c_ones[:, 0:1]
    onesrow = c_ones[0:1, :]
    hval = {}

    blk = es.enter_context(nc.Block())

    def stream_ap(dc, c0, n):
        return stream[:, dc * T + c0: dc * T + c0 + n]

    @blk.gpsimd
    def _(g):
        for dst, src in [(c_tri, i_tri), (c_ones, i_ones),
                         (v_lnb, i_vec["lnb_c"]), (v_bvr, i_bvr)]:
            s_gd.inc(g.dma_start(out=dst[:], in_=src[:]), 16)
        for n in ["mur_c", "muk_c", "bq_c", "bk_c", "bo_c", "lnw_c"]:
            s_gd.inc(g.dma_start(out=vsb[n][:], in_=i_vec[n][:]), 16)
        for dst, src in [(c_id, i_id), (c_init, i_init), (v_lamr, i_lam),
                         (hs, i_hflag), (c_onesf, i_ones[0:1, :])]:
            s_gd.inc(g.dma_start(out=dst[:], in_=src[:]), 16)
        for dc in range(NDC):
            s_gd.inc(g.dma_start(out=wg_f[:, dc * NE:(dc + 1) * NE],
                                 in_=i_wgT[dc * 128:(dc + 1) * 128, :]), 16)
        g.wait_ge(s_gd.h, s_gd.n)
        r = g.alloc_register("hreg")
        g.reg_load(r, hs[0:1, 0:1])
        hval["off"] = g.compute_val(r)
        r2 = g.alloc_register("hreg2")
        g.reg_alu(r2, r, 0, ALU.add)
        hval["off2"] = g.compute_val(r2)
        for i in range(4):
            ri = g.alloc_register("hreg4_%d" % i)
            g.reg_alu(ri, r, 4, ALU.mult)
            g.reg_alu(ri, ri, i, ALU.add)
            hval["blk%d" % i] = g.compute_val(ri)

    p0_gd = s_gd.n

    @blk.vector
    def _(v):
        v.wait_ge(s_gd.h, p0_gd)
        v.tensor_copy(wg_hi[:], wg_f[:])
        v.tensor_mul(rowtmp[0:1, 0:128], v_lamr[0:1, 0:128],
                     v_lamr[0:1, 128:256])
        v.tensor_mul(rowtmp[0:1, 128:256], v_lamr[0:1, 256:384],
                     v_lamr[0:1, 384:512])
        v.tensor_reduce(v_lsc[0:1, 0:1], rowtmp[0:1, 0:128], AX.X, ALU.add)
        v.tensor_sub(wg_t1[:], wg_f[:], wg_hi[:])
        v.tensor_reduce(v_lsc[0:1, 1:2], rowtmp[0:1, 128:256], AX.X, ALU.add)
        v.tensor_scalar_mul(v_bqs[:], vsb["bq_c"][:], QSCALE)
        v.tensor_copy(wg_mid[:], wg_t1[:])
        v.tensor_scalar_mul(v_lnw02[:], vsb["lnw_c"][:], 0.2)
        v.tensor_scalar_mul(rowtmp2[0:1, :], rowtmp[0:1, :], 1.0)
        v.tensor_sub(wg_t1[:], wg_t1[:], wg_mid[:])
        v.tensor_scalar_mul(rowtmp2[0:1, :], rowtmp[0:1, :], 1.0)
        v.tensor_scalar_mul(rowtmp2[0:1, :], rowtmp[0:1, :], 1.0)
        s_ve.inc(v.tensor_copy(wg_lo[:], wg_t1[:]), 1)

    p0_ve = s_ve.n

    @blk.scalar
    def _(s):
        s.wait_ge(s_ve.h, p0_ve)
        s_se.inc(s.activation(v_lsc[0:1, 2:4], v_lsc[0:1, 0:2], AF.Exp), 1)

    p0_se = s_se.n

    @blk.vector
    def _(v):
        v.wait_ge(s_se.h, p0_se)
        s_ve.inc(v.tensor_sub(v_lsc[0:1, 4:5], v_lsc[0:1, 3:4],
                              v_lsc[0:1, 2:3]), 1)

    p0_ve2 = s_ve.n
    with nc.psum_tensor("ps_p0", [128, 1], F32) as ps_p0:
        @blk.tensor
        def _(t):
            t.wait_ge(s_ve.h, p0_ve2)
            s_mm.inc(t.matmul(ps_p0[:], c_onesf[0:1, :], v_lsc[0:1, 4:5],
                              start=True, stop=True), 1)

        p0_mm = s_mm.n

        @blk.vector
        def _(v):
            v.wait_ge(s_mm.h, p0_mm)
            s_ve.inc(v.tensor_scalar_add(neglam[:], ps_p0[:], -LAM_INIT), 1)

    if d_lam is not None:
        pdl = snap()

        @blk.sync
        def _(sy):
            bar(sy, pdl)
            s_sd.inc(sy.dma_start(out=d_lam[0:1, 0:8], in_=v_lsc[:]), 16)
            s_sd.inc(sy.dma_start(out=d_lam[:, 8:9], in_=neglam[:], allow_slow_non_contiguous=True), 16)

    def emit_ln_stats(tag, src_ap, nblk, blk_off, xloader=None, nload=0):
        pre = snap()
        ps_es = ExitStack()
        ps_m = ps_es.enter_context(nc.psum_tensor(tag + "_m", [1, 512], F32))
        ps_m2 = ps_es.enter_context(nc.psum_tensor(tag + "_m2", [1, 512],
                                                   F32))
        if xloader is not None:
            @blk.gpsimd
            def _(g):
                bar(g, pre)
                for i in range(nblk):
                    if i >= 2:
                        s_se.wait(g, pre[3] + 3 * (i - 1))
                        s_mm.wait(g, pre[2] + 2 * (i - 1))
                    xloader(g, i)

        @blk.scalar
        def _(s):
            bar(s, pre)
            for i in range(nblk):
                if xloader is not None:
                    s_gd.wait(s, pre[0] + 16 * nload * (i + 1))
                if i >= 1:
                    s_mm.wait(s, pre[2] + 2 * i)
                for dc in range(NDC):
                    ins = s.activation(lnx_sq[:, dc * 512:(dc + 1) * 512],
                                       src_ap(i, dc), AF.Square)
                s_se.inc(ins, 1)
                s_ve.wait(s, pre[4] + 7 * i + 5)
                s.activation(rowtmp2[0:1, :], rowtmp[0:1, :], AF.Ln)
                ins = s.activation(rowr[0:1, :], rowtmp2[0:1, :],
                                   AF.Exp, scale=-0.5)
                s_se.inc(ins, 2)

        @blk.tensor
        def _(t):
            bar(t, pre)
            for i in range(nblk):
                s_se.wait(t, pre[3] + 3 * i + 1)
                if i >= 1:
                    s_ve.wait(t, pre[4] + 7 * i)
                for dc in range(NDC):
                    ins = t.matmul(ps_m[:], onescol, src_ap(i, dc),
                                   start=(dc == 0), stop=(dc == NDC - 1))
                s_mm.inc(ins, 1)
                for dc in range(NDC):
                    ins = t.matmul(ps_m2[:], onescol,
                                   lnx_sq[:, dc * 512:(dc + 1) * 512],
                                   start=(dc == 0), stop=(dc == NDC - 1))
                s_mm.inc(ins, 1)

        @blk.vector
        def _(v):
            bar(v, pre)
            for i in range(nblk):
                co = (blk_off + i) * 512
                s_mm.wait(v, pre[2] + 2 * (i + 1))
                v.tensor_scalar_mul(rowm[0:1, :], ps_m[:], 1.0 / D)
                v.tensor_scalar_mul(rowtmp[0:1, :], ps_m2[:], 1.0 / D)
                v.tensor_mul(rowtmp2[0:1, :], rowm[0:1, :], rowm[0:1, :])
                v.tensor_sub(rowtmp[0:1, :], rowtmp[0:1, :], rowtmp2[0:1, :])
                ins = v.tensor_scalar_add(rowtmp[0:1, :], rowtmp[0:1, :],
                                          EPS_LN)
                s_ve.inc(ins, 5)
                v.tensor_copy(mb_rows[0:1, co:co + 512], rowm[0:1, :])
                s_se.wait(v, pre[3] + 3 * (i + 1))
                ins = v.tensor_copy(rb_rows[0:1, co:co + 512], rowr[0:1, :])
                s_ve.inc(ins, 2)

        ps_es.close()
        assert s_se.n == pre[3] + 3 * nblk
        assert s_mm.n == pre[2] + 2 * nblk
        assert s_ve.n == pre[4] + 7 * nblk

    p1scope = ExitStack()
    x_bf = sb(p1scope, "x_bf", [128, 2 * NDC * 512], BF16)

    def xbf_ap(i, dc):
        s0 = ((i % 2) * NDC + dc) * 512
        return x_bf[:, s0:s0 + 512]

    def x_loader(g, i):
        for dc in range(NDC):
            s_gd.inc(g.dma_start(
                out=xbf_ap(i, dc),
                in_=i_x[dc * 128:(dc + 1) * 128, i * 512:(i + 1) * 512]), 16)

    emit_ln_stats("p1s", xbf_ap, NB, 0, xloader=x_loader, nload=NDC)

    if d_mr is not None:
        pmr = snap()

        @blk.sync
        def _(sy):
            bar(sy, pmr)
            s_sd.inc(sy.dma_start(out=d_mr[0:1, :], in_=mb_rows[:]), 16)
            s_sd.inc(sy.dma_start(out=d_mr[1:2, :], in_=rb_rows[:]), 16)

    if upto == "p1s":
        p1scope.close()
        fin0 = snap()

        @blk.sync
        def _(sy):
            bar(sy, fin0)
            sy.wait_ge(s_sd.h, s_sd.n)
        es.close()
        return nc, dbg_t
    w_big1 = sb(p1scope, "w_big1", [128, 2 * NDC * 1024], BF16)
    xln_b1 = sb(p1scope, "xln_b1", [128, 2 * NDC * 512], BF16)
    k_tm = sb(p1scope, "k_tm", [128, 2 * 1024], BF16)
    v_tm = sb(p1scope, "v_tm", [128, 2 * 1024], BF16)
    kv_tm = sb(p1scope, "kv_tm", [128, 2 * 1024], BF16)
    cumk_s = sb(p1scope, "cumk_s", [128, 2 * 1024], F32)
    cumkv_s = sb(p1scope, "cumkv_s", [128, 2 * 1024], F32)
    reck_s = sb(p1scope, "reck_s", [128, 1024], F32)
    s_tile = sb(p1scope, "s_tile", [128, 2 * 1024], BF16)

    def wslot1(j, rc, c0, n):
        return w_big1[:, (j * NDC + rc) * 1024 + c0:
                      (j * NDC + rc) * 1024 + c0 + n]

    def xln1_ap(i, dc, c0, n):
        s0 = ((i % 2) * NDC + dc) * 512
        return xln_b1[:, s0 + c0:s0 + c0 + n]

    p1 = snap()
    P1_GD = 16 * NDC
    P1_MM = 2 + 4 * 12
    P1_SE = 4 * 4
    P1_VE = 2 + 16 + 4 * 19
    P1_SD = 16 * 4

    def p1_mm_tA(i):
        return p1[2] + P1_MM * i + 1

    def p1_mm_tB(i):
        return p1[2] + P1_MM * i + 2

    def p1_mm_K(i, tt, ec):
        return p1[2] + P1_MM * i + 2 + 12 * tt + (ec + 1)

    def p1_mm_V(i, tt, ec):
        return p1[2] + P1_MM * i + 2 + 12 * tt + 2 + (ec + 1)

    def p1_mm_cum(i, tt, ec):
        return p1[2] + P1_MM * i + 2 + 12 * tt + 4 + (ec + 1)

    def p1_se_K(g, ec):
        return p1[3] + P1_SE * (g // 4) + 4 * (g % 4) + (ec + 1)

    def p1_se_V(g, ec):
        return p1[3] + P1_SE * (g // 4) + 4 * (g % 4) + 2 + (ec + 1)

    def p1_ve_m(i):
        return p1[4] + P1_VE * i + 1

    def p1_ve_r(i):
        return p1[4] + P1_VE * i + 2

    def p1_ve_xln(i):
        return p1[4] + P1_VE * i + 18

    def p1_ve_kv(g):
        return p1[4] + P1_VE * (g // 4) + 18 + 19 * (g % 4) + 1

    def p1_ve_cumpair(g, ec):
        return p1_ve_kv(g) + (ec + 1)

    def p1_ve_cumdone(g):
        return p1_ve_kv(g) + 16

    def p1_ve_S(g):
        return p1_ve_kv(g) + 18

    def p1_gd_w():
        return p1[0] + 16 * 16

    def p1_gd_x(i):
        return p1_gd_w() + P1_GD * (i + 1)

    def p1_sd_S(g):
        return p1[1] + 16 * (g + 1)

    @blk.gpsimd
    def _(g):
        bar(g, p1)
        for j, wn in enumerate(["w0k", "w0v"]):
            for rc in range(NDC):
                s_gd.inc(g.dma_start(out=wslot1(j, rc, 0, 1024),
                                     in_=iw[wn][rc * 128:(rc + 1) * 128, :]),
                         16)
        for i in range(NB):
            if i >= 2:
                s_ve.wait(g, p1_ve_xln(i - 1))
            x_loader(g, i)

    ps1 = ExitStack()
    ps_b = ps1.enter_context(nc.psum_tensor("p1b", [128, 512], F32))
    ps_kv = ps1.enter_context(nc.psum_tensor("p1kv", [128, 1024], F32))
    ps_ck = ps1.enter_context(nc.psum_tensor("p1ck", [128, 1024], F32))
    ps_cv = ps1.enter_context(nc.psum_tensor("p1cv", [128, 1024], F32))

    @blk.tensor
    def _(t):
        bar(t, p1)
        t.wait_ge(s_gd.h, p1_gd_w())
        for i in range(NB):
            if i >= 1:
                s_ve.wait(t, p1_ve_r(i - 1))
            s_mm.inc(t.matmul(ps_b[:], onesrow,
                              mb_rows[0:1, i * 512:i * 512 + 512],
                              start=True, stop=True), 1)
            s_ve.wait(t, p1_ve_m(i))
            s_mm.inc(t.matmul(ps_b[:], onesrow,
                              rb_rows[0:1, i * 512:i * 512 + 512],
                              start=True, stop=True), 1)
            for tt in range(4):
                g = 4 * i + tt
                if tt == 0:
                    s_ve.wait(t, p1_ve_xln(i))
                if g >= 1:
                    s_se.wait(t, p1_se_V(g - 1, 1))
                for ec in range(2):
                    for dc in range(NDC):
                        ins = t.matmul(
                            ps_kv[:, ec * 512:(ec + 1) * 512],
                            xln1_ap(i, dc, tt * 128, 128),
                            wslot1(0, dc, ec * 512, 512),
                            start=(dc == 0), stop=(dc == NDC - 1))
                    s_mm.inc(ins, 1)
                s_se.wait(t, p1_se_K(g, 1))
                for ec in range(2):
                    for dc in range(NDC):
                        ins = t.matmul(
                            ps_kv[:, ec * 512:(ec + 1) * 512],
                            xln1_ap(i, dc, tt * 128, 128),
                            wslot1(1, dc, ec * 512, 512),
                            start=(dc == 0), stop=(dc == NDC - 1))
                    s_mm.inc(ins, 1)
                s_ve.wait(t, p1_ve_kv(g))
                if g >= 1:
                    s_ve.wait(t, p1_ve_cumdone(g - 1))
                for ec in range(NDC):
                    t.matmul(ps_ck[:, ec * 128:ec * 128 + 128],
                             k_tm[:, (g % 2) * 1024 + ec * 128:
                                  (g % 2) * 1024 + ec * 128 + 128],
                             c_tri[:], start=True, stop=True)
                    ins = t.matmul(ps_cv[:, ec * 128:ec * 128 + 128],
                                   kv_tm[:, (g % 2) * 1024 + ec * 128:
                                         (g % 2) * 1024 + ec * 128 + 128],
                                   c_tri[:], start=True, stop=True)
                    s_mm.inc(ins, 1)
        s_ve.wait(t, p1_ve_S(NTT - 1))
        s_se.wait(t, p1[3] + P1_SE * NB)

    assert s_mm.n == p1[2] + P1_MM * NB

    @blk.scalar
    def _(s):
        bar(s, p1)
        for i in range(NB):
            for tt in range(4):
                g = 4 * i + tt
                for ec in range(2):
                    s_mm.wait(s, p1_mm_K(i, tt, ec))
                    ins = s.activation(
                        k_tm[:, (g % 2) * 1024 + ec * 512:
                             (g % 2) * 1024 + ec * 512 + 512],
                        ps_kv[:, ec * 512:(ec + 1) * 512], AF.Exp)
                    s_se.inc(ins, 1)
                for ec in range(2):
                    s_mm.wait(s, p1_mm_V(i, tt, ec))
                    ins = s.activation(
                        v_tm[:, (g % 2) * 1024 + ec * 512:
                             (g % 2) * 1024 + ec * 512 + 512],
                        ps_kv[:, ec * 512:(ec + 1) * 512], AF.Copy)
                    s_se.inc(ins, 1)

    assert s_se.n == p1[3] + P1_SE * NB

    @blk.vector
    def _(v):
        bar(v, p1)
        for i in range(NB):
            s_mm.wait(v, p1_mm_tA(i))
            s_ve.inc(v.tensor_copy(mbt[:], ps_b[:]), 1)
            s_mm.wait(v, p1_mm_tB(i))
            s_ve.inc(v.tensor_copy(rbt[:], ps_b[:]), 1)
            s_gd.wait(v, p1_gd_x(i))
            for dc in range(NDC):
                v.tensor_sub(xln1_ap(i, dc, 0, 512), xbf_ap(i, dc), mbt[:])
                ins = v.tensor_mul(xln1_ap(i, dc, 0, 512),
                                   xln1_ap(i, dc, 0, 512), rbt[:])
                s_ve.inc(ins, 2)
            for tt in range(4):
                g = 4 * i + tt
                s_se.wait(v, p1_se_V(g, 1))
                sl = (g % 2) * 1024
                ins = v.tensor_mul(kv_tm[:, sl:sl + 1024],
                                   k_tm[:, sl:sl + 1024],
                                   v_tm[:, sl:sl + 1024])
                s_ve.inc(ins, 1)
                s_mm.wait(v, p1_mm_cum(i, tt, 7))
                for ec in range(NDC):
                    carK = (c_init[:, 0:1] if g == 0 else
                            cumk_s[:, ((g - 1) % 2) * 1024 + ec * 128 + 127:
                                   ((g - 1) % 2) * 1024 + ec * 128 + 128])
                    carV = (c_init[:, 1:2] if g == 0 else
                            cumkv_s[:, ((g - 1) % 2) * 1024 + ec * 128 + 127:
                                    ((g - 1) % 2) * 1024 + ec * 128 + 128])
                    v.tensor_scalar_add(
                        cumk_s[:, sl + ec * 128:sl + ec * 128 + 128],
                        ps_ck[:, ec * 128:ec * 128 + 128], carK)
                    ins = v.tensor_scalar_add(
                        cumkv_s[:, sl + ec * 128:sl + ec * 128 + 128],
                        ps_cv[:, ec * 128:ec * 128 + 128], carV)
                    s_ve.inc(ins, 2)
                if g >= 2:
                    s_sd.wait(v, p1_sd_S(g - 2))
                v.reciprocal(reck_s[:], cumk_s[:, sl:sl + 1024])
                ins = v.tensor_mul(s_tile[:, (g % 2) * 1024:
                                          (g % 2) * 1024 + 1024],
                                   cumkv_s[:, sl:sl + 1024], reck_s[:])
                s_ve.inc(ins, 2)

    assert s_ve.n == p1[4] + P1_VE * NB

    @blk.sync
    def _(sy):
        bar(sy, p1)
        for g in range(NTT):
            s_ve.wait(sy, p1_ve_S(g))
            s_sd.inc(sy.dma_start(out=s_shbm[:, g * 1024:(g + 1) * 1024],
                                  in_=s_tile[:, (g % 2) * 1024:
                                             (g % 2) * 1024 + 1024]), 16)

    assert s_sd.n == p1[1] + P1_SD * NB
    ps1.close()
    p1scope.close()

    if d_S is not None:
        pdS = snap()
        dsb_es = ExitStack()
        dsb = sb(dsb_es, "dsb", [128, 1024], BF16)

        @blk.sync
        def _(sy):
            bar(sy, pdS)
            for g in range(NTT):
                s_sd.inc(sy.dma_start(out=dsb[:],
                                      in_=s_shbm[:, g * 1024:(g + 1) * 1024]),
                         16)
                sy.wait_ge(s_sd.h, s_sd.n)
                s_sd.inc(sy.dma_start(out=d_S[:, g * 1024:(g + 1) * 1024],
                                      in_=dsb[:]), 16)
                sy.wait_ge(s_sd.h, s_sd.n)
        dsb_es.close()

    if upto != "p1":
        p2scope = ExitStack()
        w_big2 = sb(p2scope, "w_big2", [128, 2 * NDC * 1024], BF16)
        x_bf2 = sb(p2scope, "x_bf2", [128, 2 * NDC * 512], BF16)
        xln_b2 = sb(p2scope, "xln_b2", [128, 2 * NDC * 512], BF16)
        qt_sb = sb(p2scope, "qt_sb", [128, NDC * 512], BF16)
        o_sb = sb(p2scope, "o_sb", [128, NDC * 512], BF16)
        s_in = sb(p2scope, "s_in", [128, 2 * 4096], BF16)

        def wslot2(j, rc, c0, n):
            return w_big2[:, (j * NDC + rc) * 1024 + c0:
                          (j * NDC + rc) * 1024 + c0 + n]

        def xf_ap(i, dc):
            s0 = ((i % 2) * NDC + dc) * 512
            return x_bf2[:, s0:s0 + 512]

        def xln2_ap(i, dc, c0, n):
            s0 = ((i % 2) * NDC + dc) * 512
            return xln_b2[:, s0 + c0:s0 + c0 + n]

        p2 = snap()
        P2_GD = 16 * NDC
        P2_MM = 1 + 8 + 8
        P2_SE = 8
        P2_VE = 2 + 16 + 1 + 8 + 8
        P2_SD = 16

        def p2_sd_S(i):
            return p2[1] + P2_SD * (i + 1)

        def p2_gd_x(i):
            return p2[0] + 16 * 16 + P2_GD * (i + 1)

        def p2_mm_t1(i):
            return p2[2] + P2_MM * i + 1

        def p2_mm_q(i, ec):
            return p2_mm_t1(i) + (ec + 1)

        def p2_mm_o(i, dc):
            return p2_mm_t1(i) + 8 + (dc + 1)

        def p2_se_q(i, ec):
            return p2[3] + P2_SE * i + (ec + 1)

        def p2_ve_v1(i):
            return p2[4] + P2_VE * i + 2

        def p2_ve_xln(i):
            return p2[4] + P2_VE * i + 18

        def p2_ve_Q(i):
            return p2[4] + P2_VE * i + 19

        def p2_ve_O(i, ec):
            return p2_ve_Q(i) + (ec + 1)

        def p2_ve_res(i, dc):
            return p2_ve_Q(i) + 8 + (dc + 1)

        @blk.gpsimd
        def _(g):
            bar(g, p2)
            for j, wn in enumerate(["w0q", "w0c"]):
                for rc in range(NDC):
                    s_gd.inc(g.dma_start(
                        out=wslot2(j, rc, 0, 1024),
                        in_=iw[wn][rc * 128:(rc + 1) * 128, :]), 16)
            for i in range(NB):
                if i >= 2:
                    s_ve.wait(g, p2_ve_res(i - 1, 7))
                for dc in range(NDC):
                    s_gd.inc(g.dma_start(
                        out=xf_ap(i, dc),
                        in_=i_x[dc * 128:(dc + 1) * 128,
                                i * 512:(i + 1) * 512]), 16)

        @blk.sync
        def _(sy):
            bar(sy, p2)
            for i in range(NB):
                if i >= 2:
                    s_ve.wait(sy, p2_ve_O(i - 2, 7))
                s_sd.inc(sy.dma_start(
                    out=s_in[:, (i % 2) * 4096:(i % 2) * 4096 + 4096],
                    in_=s_shbm[:, i * 4096:(i + 1) * 4096]), 16)

        ps2 = ExitStack()
        ps_bm2 = ps2.enter_context(nc.psum_tensor("p2bm", [128, 512], F32))
        ps_br2 = ps2.enter_context(nc.psum_tensor("p2br", [128, 512], F32))
        ps_q = ps2.enter_context(nc.psum_tensor("p2q", [128, 2 * 512], F32))
        ps_o = ps2.enter_context(nc.psum_tensor("p2o", [128, 2 * 512], F32))

        @blk.tensor
        def _(t):
            bar(t, p2)
            t.wait_ge(s_gd.h, p2[0] + 16 * 16)
            for i in range(NB):
                if i >= 1:
                    s_ve.wait(t, p2_ve_v1(i - 1))
                t.matmul(ps_bm2[:], onesrow,
                         mb_rows[0:1, i * 512:i * 512 + 512],
                         start=True, stop=True)
                s_mm.inc(t.matmul(ps_br2[:], onesrow,
                                  rb_rows[0:1, i * 512:i * 512 + 512],
                                  start=True, stop=True), 1)
                s_ve.wait(t, p2_ve_xln(i))
                for ec in range(NDC):
                    if ec >= 2:
                        s_se.wait(t, p2_se_q(i, ec - 2))
                    elif i >= 1:
                        s_se.wait(t, p2_se_q(i - 1, 6 + ec))
                    sl = (ec % 2) * 512
                    for dc in range(NDC):
                        ins = t.matmul(ps_q[:, sl:sl + 512],
                                       wslot2(0, dc, ec * 128, 128),
                                       xln2_ap(i, dc, 0, 512),
                                       start=(dc == 0), stop=(dc == NDC - 1))
                    s_mm.inc(ins, 1)
                for dc in range(NDC):
                    if dc >= 2:
                        s_ve.wait(t, p2_ve_res(i, dc - 2))
                    elif i >= 1:
                        s_ve.wait(t, p2_ve_res(i - 1, 6 + dc))
                    if dc == 0:
                        s_ve.wait(t, p2_ve_O(i, 7))
                    sl = (dc % 2) * 512
                    for ec in range(NDC):
                        ins = t.matmul(ps_o[:, sl:sl + 512],
                                       wslot2(1, ec, dc * 128, 128),
                                       o_sb[:, ec * 512:(ec + 1) * 512],
                                       start=(ec == 0), stop=(ec == NDC - 1))
                    s_mm.inc(ins, 1)
            s_ve.wait(t, p2[4] + P2_VE * NB)

        @blk.scalar
        def _(s):
            bar(s, p2)
            for i in range(NB):
                for ec in range(NDC):
                    s_mm.wait(s, p2_mm_q(i, ec))
                    sl = (ec % 2) * 512
                    ins = s.activation(qt_sb[:, ec * 512:(ec + 1) * 512],
                                       ps_q[:, sl:sl + 512], AF.Tanh,
                                       scale=0.5)
                    s_se.inc(ins, 1)

        @blk.vector
        def _(v):
            bar(v, p2)
            for i in range(NB):
                s_mm.wait(v, p2_mm_t1(i))
                v.tensor_copy(mbt[:], ps_bm2[:])
                s_ve.inc(v.tensor_copy(rbt[:], ps_br2[:]), 2)
                s_gd.wait(v, p2_gd_x(i))
                for dc in range(NDC):
                    v.tensor_sub(xln2_ap(i, dc, 0, 512), xf_ap(i, dc),
                                 mbt[:])
                    ins = v.tensor_mul(xln2_ap(i, dc, 0, 512),
                                       xln2_ap(i, dc, 0, 512), rbt[:])
                    s_ve.inc(ins, 2)
                s_se.wait(v, p2_se_q(i, 7))
                ins = v.tensor_scalar(qt_sb[:], qt_sb[:], 0.5, 0.5,
                                      ALU.mult, ALU.add)
                s_ve.inc(ins, 1)
                s_sd.wait(v, p2_sd_S(i))
                for ec in range(NDC):
                    s_ap = s_in[:, (i % 2) * 4096:(i % 2) * 4096 + 4096]
                    s_ap = s_ap.rearrange("p (tt e) -> p tt e", tt=4)
                    s_ap = s_ap[:, :, ec * 128:(ec + 1) * 128]
                    ins = v.tensor_mul(
                        o_sb[:, ec * 512:(ec + 1) * 512].rearrange(
                            "p (tt q) -> p tt q", tt=4),
                        s_ap,
                        qt_sb[:, ec * 512:(ec + 1) * 512].rearrange(
                            "p (tt q) -> p tt q", tt=4))
                    s_ve.inc(ins, 1)
                for dc in range(NDC):
                    s_mm.wait(v, p2_mm_o(i, dc))
                    sl = (dc % 2) * 512
                    ins = v.tensor_add(stream_ap(dc, i * 512, 512),
                                       xf_ap(i, dc), ps_o[:, sl:sl + 512])
                    s_ve.inc(ins, 1)

        assert s_ve.n == p2[4] + P2_VE * NB
        assert s_mm.n == p2[2] + P2_MM * NB
        ps2.close()
        p2scope.close()

        if d_stream is not None and upto == "p2":
            pd = snap()

            @blk.sync
            def _(sy):
                bar(sy, pd)
                for dc in range(NDC):
                    s_sd.inc(sy.dma_start(
                        out=d_stream[:, dc * T:(dc + 1) * T],
                        in_=stream[:, dc * T:(dc + 1) * T]), 16)

    if upto not in ("p1", "p2"):
        emit_ln_stats("p3s", lambda i, dc: stream_ap(dc, i * 512, 512), NB, 0)

        p3scope = ExitStack()
        w_big3 = sb(p3scope, "w_big3", [128, 3 * NDC * 1024], BF16)
        xln2p = sb(p3scope, "xln2p", [128, 2 * NDC * 513], BF16)
        xr_sb = sb(p3scope, "xr_sb", [128, NDC * 512], BF16)
        xk_sb = sb(p3scope, "xk_sb", [128, NDC * 512], BF16)
        r_fm = sb(p3scope, "r_fm", [128, NDC * 512], BF16)
        krelu = sb(p3scope, "krelu", [128, NDC * 512], BF16)
        tmp_o = sb(p3scope, "tmp_o", [128, 512], F32)

        def wslot3(j, rc, c0, n):
            return w_big3[:, (j * NDC + rc) * 1024 + c0:
                          (j * NDC + rc) * 1024 + c0 + n]

        def xp_cur(i, dc):
            s0 = ((i % 2) * NDC + dc) * 513
            return xln2p[:, s0 + 1:s0 + 513]

        def xp_prev(i, dc):
            s0 = ((i % 2) * NDC + dc) * 513
            return xln2p[:, s0:s0 + 512]

        p3 = snap()
        P3_MM = 26
        P3_SE = 16
        P3_VE0 = 0
        P3_VE = 74
        P3_SD = 16 * 8

        def p3_mm_tA(i):
            return p3[2] + P3_MM * i + 1

        def p3_mm_tB(i):
            return p3[2] + P3_MM * i + 2

        def p3_mm_g(i, n):
            return p3[2] + P3_MM * i + 2 + (n + 1)

        def p3_se_r(i, j):
            return p3[3] + P3_SE * i + (j + 1)

        def p3_se_k(i, j):
            return p3[3] + P3_SE * i + 8 + (j + 1)

        def p3_ve(i, pos):
            return p3[4] + P3_VE0 + P3_VE * i + pos

        @blk.gpsimd
        def _(g):
            bar(g, p3)
            for j, wn in enumerate(["w0fr", "w0fk", "w0fv"]):
                for rc in range(NDC):
                    s_gd.inc(g.dma_start(
                        out=wslot3(j, rc, 0, 1024),
                        in_=iw[wn][rc * 128:(rc + 1) * 128, :]), 16)

        ps3 = ExitStack()
        ps_b3 = ps3.enter_context(nc.psum_tensor("p3b", [128, 512], F32))
        ps_p3 = ps3.enter_context(nc.psum_tensor("p3p", [128, 1024], F32))

        @blk.tensor
        def _(t):
            bar(t, p3)
            t.wait_ge(s_gd.h, p3[0] + 16 * 24)
            for i in range(NB):
                if i >= 1:
                    s_ve.wait(t, p3_ve(i - 1, 2))
                s_mm.inc(t.matmul(ps_b3[:], onesrow,
                                  mb_rows[0:1, i * 512:i * 512 + 512],
                                  start=True, stop=True), 1)
                s_ve.wait(t, p3_ve(i, 1))
                s_mm.inc(t.matmul(ps_b3[:], onesrow,
                                  rb_rows[0:1, i * 512:i * 512 + 512],
                                  start=True, stop=True), 1)
                for n in range(24):
                    j = n % 8
                    w = n // 8
                    if n == 0:
                        s_ve.wait(t, p3_ve(i, 50))
                    if n == 16:
                        s_ve.wait(t, p3_ve(i, 58))
                    if n >= 2:
                        pn = n - 2
                        if pn < 8:
                            s_se.wait(t, p3_se_r(i, pn))
                        elif pn < 16:
                            s_se.wait(t, p3_se_k(i, pn - 8))
                        else:
                            s_ve.wait(t, p3_ve(i, 58 + 2 * (pn - 16) + 1))
                    elif i >= 1:
                        pn = 22 + n
                        s_ve.wait(t, p3_ve(i - 1, 58 + 2 * (pn - 16) + 1))
                    rhs = (xr_sb if w == 0 else
                           xk_sb if w == 1 else krelu)
                    sl = (n % 2) * 512
                    for cc in range(NDC):
                        ins = t.matmul(
                            ps_p3[:, sl:sl + 512],
                            wslot3(w, cc, j * 128, 128),
                            rhs[:, cc * 512:(cc + 1) * 512],
                            start=(cc == 0), stop=(cc == NDC - 1))
                    s_mm.inc(ins, 1)
            s_ve.wait(t, p3[4] + P3_VE0 + P3_VE * NB)
            s_se.wait(t, p3[3] + P3_SE * NB)

        assert s_mm.n == p3[2] + P3_MM * NB

        @blk.scalar
        def _(s):
            bar(s, p3)
            for i in range(NB):
                for j in range(8):
                    s_mm.wait(s, p3_mm_g(i, j))
                    sl = (j % 2) * 512
                    ins = s.activation(r_fm[:, j * 512:(j + 1) * 512],
                                       ps_p3[:, sl:sl + 512], AF.Sigmoid)
                    s_se.inc(ins, 1)
                for j in range(8):
                    s_mm.wait(s, p3_mm_g(i, 8 + j))
                    sl = (j % 2) * 512
                    ins = s.activation(krelu[:, j * 512:(j + 1) * 512],
                                       ps_p3[:, sl:sl + 512], AF.Relu)
                    s_se.inc(ins, 1)

        assert s_se.n == p3[3] + P3_SE * NB

        @blk.vector
        def _(v):
            bar(v, p3)
            for i in range(NB):
                s_mm.wait(v, p3_mm_tA(i))
                s_ve.inc(v.tensor_copy(mbt[:], ps_b3[:]), 1)
                s_mm.wait(v, p3_mm_tB(i))
                s_ve.inc(v.tensor_copy(rbt[:], ps_b3[:]), 1)
                for dc in range(NDC):
                    v.tensor_sub(xp_cur(i, dc), stream_ap(dc, i * 512, 512),
                                 mbt[:])
                    ins = v.tensor_mul(xp_cur(i, dc), xp_cur(i, dc), rbt[:])
                    s_ve.inc(ins, 2)
                if i >= 1:
                    for dc in range(NDC):
                        s0 = ((i % 2) * NDC + dc) * 513
                        s1 = (((i - 1) % 2) * NDC + dc) * 513
                        ins = v.tensor_copy(xln2p[:, s0:s0 + 1],
                                            xln2p[:, s1 + 512:s1 + 513])
                        s_ve.inc(ins, 1)
                else:
                    for dc in range(NDC):
                        s_ve.inc(v.memset(xln2p[:, dc * 513:dc * 513 + 1],
                                          0.0), 1)
                for dc in range(NDC):
                    ins = v.tensor_sub(xk_sb[:, dc * 512:(dc + 1) * 512],
                                       xp_cur(i, dc), xp_prev(i, dc))
                    s_ve.inc(ins, 1)
                for dc in range(NDC):
                    ins = v.scalar_tensor_tensor(
                        xr_sb[:, dc * 512:(dc + 1) * 512],
                        xk_sb[:, dc * 512:(dc + 1) * 512],
                        vsb["mur_c"][:, dc:dc + 1],
                        xp_prev(i, dc), ALU.mult, ALU.add)
                    s_ve.inc(ins, 1)
                for dc in range(NDC):
                    ins = v.scalar_tensor_tensor(
                        xk_sb[:, dc * 512:(dc + 1) * 512],
                        xk_sb[:, dc * 512:(dc + 1) * 512],
                        vsb["muk_c"][:, dc:dc + 1],
                        xp_prev(i, dc), ALU.mult, ALU.add)
                    s_ve.inc(ins, 1)
                for j in range(8):
                    s_se.wait(v, p3_se_k(i, j))
                    ins = v.tensor_mul(krelu[:, j * 512:(j + 1) * 512],
                                       krelu[:, j * 512:(j + 1) * 512],
                                       krelu[:, j * 512:(j + 1) * 512])
                    s_ve.inc(ins, 1)
                for j in range(8):
                    s_mm.wait(v, p3_mm_g(i, 16 + j))
                    sl = (j % 2) * 512
                    v.tensor_mul(tmp_o[:], ps_p3[:, sl:sl + 512],
                                 r_fm[:, j * 512:(j + 1) * 512])
                    ins = v.tensor_add(stream_ap(j, i * 512, 512),
                                       stream_ap(j, i * 512, 512), tmp_o[:])
                    s_ve.inc(ins, 2)

        assert s_ve.n == p3[4] + P3_VE0 + P3_VE * NB

        @blk.sync
        def _(sy):
            bar(sy, p3)
            for i in range(NB):
                for dc in range(NDC):
                    s_ve.wait(sy, p3_ve(i, 58 + 2 * (dc + 1)))
                    s_sd.inc(sy.dma_start(
                        out=s_x2hbm[dc * 128:(dc + 1) * 128,
                                    i * 512:(i + 1) * 512],
                        in_=stream_ap(dc, i * 512, 512)), 16)

        assert s_sd.n == p3[1] + P3_SD * NB
        ps3.close()
        p3scope.close()

        if d_stream is not None and upto == "p3":
            pd3 = snap()

            @blk.sync
            def _(sy):
                bar(sy, pd3)
                for dc in range(NDC):
                    s_sd.inc(sy.dma_start(
                        out=d_stream[:, dc * T:(dc + 1) * T],
                        in_=stream[:, dc * T:(dc + 1) * T]), 16)

    p45scope = ExitStack()
    if upto not in ("p1", "p2", "p3"):
        emit_ln_stats("p4s", lambda i, dc: stream_ap(dc, i * 512, 512), NB, 0)

        p4scope = ExitStack()
        w_big4 = sb(p4scope, "w_big4", [128, 2 * NDC * 1024], BF16)
        xln_b4 = sb(p4scope, "xln_b4", [128, 2 * NDC * 512], BF16)
        q_t = sb(p4scope, "q_t", [128, 2 * 512], BF16)
        k_t = sb(p4scope, "k_t", [128, 2 * 512], BF16)

        def wslot4(j, rc, c0, n):
            return w_big4[:, (j * NDC + rc) * 1024 + c0:
                          (j * NDC + rc) * 1024 + c0 + n]

        def xln4_ap(i, dc, c0, n):
            s0 = ((i % 2) * NDC + dc) * 512
            return xln_b4[:, s0 + c0:s0 + c0 + n]

        p4 = snap()
        P4_MM = 18
        P4_SE = 16
        P4_VE = 18
        P4_SD = 16 * 16

        def p4_mm_tA(i):
            return p4[2] + P4_MM * i + 1

        def p4_mm_tB(i):
            return p4[2] + P4_MM * i + 2

        def p4_mm_g(i, n):
            return p4[2] + P4_MM * i + 2 + (n + 1)

        def p4_se(i, n):
            return p4[3] + P4_SE * i + (n + 1)

        def p4_ve(i, pos):
            return p4[4] + P4_VE * i + pos

        def p4_sd(i, idx):
            return p4[1] + P4_SD * i + 16 * idx

        @blk.gpsimd
        def _(g):
            bar(g, p4)
            for j, wn in enumerate(["w1q", "w1k"]):
                for rc in range(NDC):
                    s_gd.inc(g.dma_start(
                        out=wslot4(j, rc, 0, 1024),
                        in_=iw[wn][rc * 128:(rc + 1) * 128, :]), 16)

        ps4 = ExitStack()
        ps_b4 = ps4.enter_context(nc.psum_tensor("p4b", [128, 512], F32))
        ps_q4 = ps4.enter_context(nc.psum_tensor("p4q", [128, 1024], F32))

        @blk.tensor
        def _(t):
            bar(t, p4)
            t.wait_ge(s_gd.h, p4[0] + 16 * 16)
            for i in range(NB):
                if i >= 1:
                    s_ve.wait(t, p4_ve(i - 1, 2))
                s_mm.inc(t.matmul(ps_b4[:], onesrow,
                                  mb_rows[0:1, i * 512:i * 512 + 512],
                                  start=True, stop=True), 1)
                s_ve.wait(t, p4_ve(i, 1))
                s_mm.inc(t.matmul(ps_b4[:], onesrow,
                                  rb_rows[0:1, i * 512:i * 512 + 512],
                                  start=True, stop=True), 1)
                for n in range(16):
                    w, ec = n // 8, n % 8
                    if n == 0:
                        s_ve.wait(t, p4_ve(i, 18))
                    if n >= 2:
                        s_se.wait(t, p4_se(i, n - 2))
                    elif i >= 1:
                        s_se.wait(t, p4_se(i - 1, 14 + n))
                    sl = (n % 2) * 512
                    for dc in range(NDC):
                        ins = t.matmul(ps_q4[:, sl:sl + 512],
                                       wslot4(w, dc, ec * 128, 128),
                                       xln4_ap(i, dc, 0, 512),
                                       start=(dc == 0), stop=(dc == NDC - 1))
                    s_mm.inc(ins, 1)
            s_ve.wait(t, p4[4] + P4_VE * NB)
            s_se.wait(t, p4[3] + P4_SE * NB)

        assert s_mm.n == p4[2] + P4_MM * NB

        @blk.scalar
        def _(s):
            bar(s, p4)
            for i in range(NB):
                for n in range(16):
                    w, ec = n // 8, n % 8
                    s_mm.wait(s, p4_mm_g(i, n))
                    if i + n > 0:
                        pi, pn = (i, n - 2) if n >= 2 else (i - 1, 14 + n)
                        s_sd.wait(s, p4_sd(pi, pn + 1))
                    sl = (n % 2) * 512
                    dst = q_t if w == 0 else k_t
                    bias = (v_bqs if w == 0 else vsb["bk_c"])[:, ec:ec + 1]
                    ins = s.activation(dst[:, sl:sl + 512],
                                       ps_q4[:, sl:sl + 512], AF.Identity,
                                       bias=bias,
                                       scale=QSCALE if w == 0 else 1.0)
                    s_se.inc(ins, 1)

        assert s_se.n == p4[3] + P4_SE * NB

        @blk.vector
        def _(v):
            bar(v, p4)
            for i in range(NB):
                s_mm.wait(v, p4_mm_tA(i))
                s_ve.inc(v.tensor_copy(mbt[:], ps_b4[:]), 1)
                s_mm.wait(v, p4_mm_tB(i))
                s_ve.inc(v.tensor_copy(rbt[:], ps_b4[:]), 1)
                for dc in range(NDC):
                    v.tensor_sub(xln4_ap(i, dc, 0, 512),
                                 stream_ap(dc, i * 512, 512), mbt[:])
                    ins = v.tensor_mul(xln4_ap(i, dc, 0, 512),
                                       xln4_ap(i, dc, 0, 512), rbt[:])
                    s_ve.inc(ins, 2)

        assert s_ve.n == p4[4] + P4_VE * NB

        @blk.sync
        def _(sy):
            bar(sy, p4)
            for i in range(NB):
                for n in range(16):
                    w, ec = n // 8, n % 8
                    s_se.wait(sy, p4_se(i, n))
                    sl = (n % 2) * 512
                    src = q_t if w == 0 else k_t
                    dstt = s_qhbm if w == 0 else s_khbm
                    s_sd.inc(sy.dma_start(
                        out=dstt[ec * 128:(ec + 1) * 128,
                                 i * 512:(i + 1) * 512],
                        in_=src[:, sl:sl + 512]), 16)

        assert s_sd.n == p4[1] + P4_SD * NB
        ps4.close()
        p4scope.close()

        vv_s = sb(p45scope, "vv_s", [128, NTT * 512], BF16)
        p4vscope = ExitStack()
        w_v4 = sb(p4vscope, "w_v4", [128, NDC * 1024], BF16)
        xln_v4 = sb(p4vscope, "xln_v4", [128, 2 * NDC * 512], BF16)
        v_tm4 = sb(p4vscope, "v_tm4", [128, 2 * 1024], BF16)

        def xlnv_ap(i, dc, c0, n):
            s0 = ((i % 2) * NDC + dc) * 512
            return xln_v4[:, s0 + c0:s0 + c0 + n]

        p4v = snap()
        P4V_MM = 2 + 8
        P4V_SE = 8
        P4V_VE = 2 + 16 + 4

        def p4v_mm_tA(i):
            return p4v[2] + P4V_MM * i + 1

        def p4v_mm_tB(i):
            return p4v[2] + P4V_MM * i + 2

        def p4v_mm_g(i, tt, ec):
            return p4v[2] + P4V_MM * i + 2 + 2 * tt + (ec + 1)

        def p4v_se(i, tt, ec):
            return p4v[3] + P4V_SE * i + 2 * tt + (ec + 1)

        def p4v_ve(i, pos):
            return p4v[4] + P4V_VE * i + pos

        @blk.gpsimd
        def _(g):
            bar(g, p4v)
            for rc in range(NDC):
                s_gd.inc(g.dma_start(
                    out=w_v4[:, rc * 1024:(rc + 1) * 1024],
                    in_=iw["w1v"][rc * 128:(rc + 1) * 128, :]), 16)

        ps4v = ExitStack()
        ps_b4v = ps4v.enter_context(nc.psum_tensor("p4vb", [128, 512], F32))
        ps_v4 = ps4v.enter_context(nc.psum_tensor("p4vv", [128, 1024], F32))

        @blk.tensor
        def _(t):
            bar(t, p4v)
            t.wait_ge(s_gd.h, p4v[0] + 16 * 8)
            for i in range(NB):
                if i >= 1:
                    s_ve.wait(t, p4v_ve(i - 1, 2))
                s_mm.inc(t.matmul(ps_b4v[:], onesrow,
                                  mb_rows[0:1, i * 512:i * 512 + 512],
                                  start=True, stop=True), 1)
                s_ve.wait(t, p4v_ve(i, 1))
                s_mm.inc(t.matmul(ps_b4v[:], onesrow,
                                  rb_rows[0:1, i * 512:i * 512 + 512],
                                  start=True, stop=True), 1)
                for tt in range(4):
                    g = 4 * i + tt
                    if tt == 0:
                        s_ve.wait(t, p4v_ve(i, 18))
                    for ec in range(2):
                        gn = 2 * tt + ec
                        if gn >= 2:
                            s_se.wait(t, p4v_se(i, (gn - 2) // 2,
                                                (gn - 2) % 2))
                        elif i >= 1:
                            pg = 6 + gn
                            s_se.wait(t, p4v_se(i - 1, pg // 2, pg % 2))
                        sl = ec * 512
                        for dc in range(NDC):
                            t.matmul(ps_v4[:, sl:sl + 512],
                                     xlnv_ap(i, dc, tt * 128, 128),
                                     w_v4[:, dc * 1024 + sl:
                                          dc * 1024 + sl + 512],
                                     start=(dc == 0), stop=False)
                        ins = t.matmul(ps_v4[:, sl:sl + 512],
                                       onesrow[0:1, 0:128],
                                       v_bvr[0:1, sl:sl + 512],
                                       start=False, stop=True)
                        s_mm.inc(ins, 1)
            s_ve.wait(t, p4v[4] + P4V_VE * NB)
            s_se.wait(t, p4v[3] + P4V_SE * NB)

        assert s_mm.n == p4v[2] + P4V_MM * NB

        @blk.scalar
        def _(s):
            bar(s, p4v)
            for i in range(NB):
                for tt in range(4):
                    g = 4 * i + tt
                    for ec in range(2):
                        s_mm.wait(s, p4v_mm_g(i, tt, ec))
                        if ec == 0 and g >= 2:
                            i2, tt2 = (g - 2) // 4, (g - 2) % 4
                            s_ve.wait(s, p4v_ve(i2, 18 + tt2 + 1))
                        sl = ec * 512
                        ins = s.activation(
                            v_tm4[:, (g % 2) * 1024 + sl:
                                  (g % 2) * 1024 + sl + 512],
                            ps_v4[:, sl:sl + 512], AF.Copy)
                        s_se.inc(ins, 1)

        assert s_se.n == p4v[3] + P4V_SE * NB

        @blk.vector
        def _(v):
            bar(v, p4v)
            for i in range(NB):
                s_mm.wait(v, p4v_mm_tA(i))
                s_ve.inc(v.tensor_copy(mbt[:], ps_b4v[:]), 1)
                s_mm.wait(v, p4v_mm_tB(i))
                s_ve.inc(v.tensor_copy(rbt[:], ps_b4v[:]), 1)
                for dc in range(NDC):
                    v.tensor_sub(xlnv_ap(i, dc, 0, 512),
                                 stream_ap(dc, i * 512, 512), mbt[:])
                    ins = v.tensor_mul(xlnv_ap(i, dc, 0, 512),
                                       xlnv_ap(i, dc, 0, 512), rbt[:])
                    s_ve.inc(ins, 2)
                for tt in range(4):
                    g = 4 * i + tt
                    s_se.wait(v, p4v_se(i, tt, 1))
                    sl = (g % 2) * 1024
                    ins = v.scalar_tensor_tensor(
                        vv_s[:, g * 512:(g + 1) * 512],
                        v_tm4[:, sl + 512:sl + 1024], neglam[:, 0:1],
                        v_tm4[:, sl:sl + 512], ALU.mult, ALU.add)
                    s_ve.inc(ins, 1)

        assert s_ve.n == p4v[4] + P4V_VE * NB
        ps4v.close()
        p4vscope.close()

        if d_vv is not None and upto == "p4":
            pd4 = snap()

            @blk.sync
            def _(sy):
                bar(sy, pd4)
                for g in range(NTT):
                    s_sd.inc(sy.dma_start(
                        out=d_vv[:, g * 512:(g + 1) * 512],
                        in_=vv_s[:, g * 512:(g + 1) * 512]), 16)

    if upto not in ("p1", "p2", "p3", "p4"):
        p5scope = ExitStack()
        kh = sb(p5scope, "kh", [128, 2 * 4096], BF16)
        qh = sb(p5scope, "qh", [128, 2 * 2048], BF16)
        sq_q = sb(p5scope, "sq_q", [128, 2048], BF16)
        qn2row = sb(p5scope, "qn2row", [1, 2048], BF16)
        kn2row = sb(p5scope, "kn2row", [1, 4096], BF16)
        crow = sb(p5scope, "crow", [1, 8], F32)
        ccols = sb(p5scope, "ccols", [128, 4], F32)
        pt_s = sb(p5scope, "pt_s", [128, 4 * 512], BF16)
        rl_row = sb(p5scope, "rl_row", [1, 512], F32)
        rlb_sb = sb(p5scope, "rlb_sb", [128, 512], F32)
        att_t = sb(p5scope, "att_t", [128, 2 * 512], BF16)
        sq_k = lnx_sq

        p5 = snap()
        P5_MM_H = 13 + 4 * 34
        P5_SE_H = 2 + 4 * 32
        P5_VE_H = 17 + 4 * 4
        P5_GD_H = 16

        def p5_gd_q(h):
            return p5[0] + 16 * (h + 1)

        def p5_sd_k(h):
            if h == 0:
                return p5[1] + 16
            if h == 1:
                return p5[1] + 32
            return p5[1] + 32 + 64 * (h - 2) + 16 * min(h - 2, 6) + 64 + 16

        def p5_sd_st(h, qb):
            return p5[1] + 32 + 64 * h + 16 * min(h, 6) + 16 * (qb + 1)

        def p5_se_sqq(h):
            return p5[3] + P5_SE_H * h + 1

        def p5_se_sqk(h):
            return p5[3] + P5_SE_H * h + 2

        def p5_se_exp(h, qb, kc):
            return p5[3] + P5_SE_H * h + 2 + 32 * qb + kc + 1

        def p5_mm_qn(h, c):
            return p5[2] + P5_MM_H * h + c + 1

        def p5_mm_kn(h, c):
            return p5[2] + P5_MM_H * h + 4 + c + 1

        def p5_mm_cb(h):
            return p5[2] + P5_MM_H * h + 13

        def p5_mm_s(h, qb, kc):
            return p5[2] + P5_MM_H * h + 13 + 34 * qb + kc + 1

        def p5_mm_avl(h, qb):
            return p5[2] + P5_MM_H * h + 13 + 34 * qb + 33

        def p5_mm_rlb(h, qb):
            return p5[2] + P5_MM_H * h + 13 + 34 * qb + 34

        def p5_ve_qn(h, c):
            return p5[4] + P5_VE_H * h + c + 1

        def p5_ve_kn(h, c):
            return p5[4] + P5_VE_H * h + 4 + c + 1

        def p5_ve_cmath(h):
            return p5[4] + P5_VE_H * h + 16

        def p5_ve_ccols(h):
            return p5[4] + P5_VE_H * h + 17

        def p5_ve_recip(h, qb):
            return p5[4] + P5_VE_H * h + 17 + 4 * qb + 1

        def p5_ve_rlbf(h, qb):
            return p5[4] + P5_VE_H * h + 17 + 4 * qb + 2

        def p5_ve_att(h, qb):
            return p5[4] + P5_VE_H * h + 17 + 4 * qb + 4

        @blk.gpsimd
        def _(g):
            bar(g, p5)
            for h in range(NHEADS):
                if h >= 2:
                    s_mm.wait(g, p5[2] + P5_MM_H * (h - 1))
                s_gd.inc(g.dma_start(
                    out=qh[:, (h % 2) * 2048:(h % 2) * 2048 + 2048],
                    in_=s_qhbm[h * 128:(h + 1) * 128,
                               bass.ts(hval["off"], TOWN)]), 16)

        @blk.sync
        def _(sy):
            bar(sy, p5)
            for h in range(NHEADS):
                if h >= 2:
                    s_mm.wait(sy, p5[2] + P5_MM_H * (h - 1))
                s_sd.inc(sy.dma_start(
                    out=kh[:, (h % 2) * 4096:(h % 2) * 4096 + 4096],
                    in_=s_khbm[h * 128:(h + 1) * 128, :]), 16)
                if h == 0:
                    continue
                hh = h - 1
                for qb in range(4):
                    s_ve.wait(sy, p5_ve_att(hh, qb))
                    s_sd.inc(sy.dma_start(
                        out=s_athbm[hh * 128:(hh + 1) * 128,
                                    qb * 512:(qb + 1) * 512],
                        in_=att_t[:, ((4 * hh + qb) % 2) * 512:
                                  ((4 * hh + qb) % 2) * 512 + 512]), 16)
            hh = NHEADS - 1
            for qb in range(4):
                s_ve.wait(sy, p5_ve_att(hh, qb))
                s_sd.inc(sy.dma_start(
                    out=s_athbm[hh * 128:(hh + 1) * 128,
                                qb * 512:(qb + 1) * 512],
                    in_=att_t[:, ((4 * hh + qb) % 2) * 512:
                              ((4 * hh + qb) % 2) * 512 + 512]), 16)

        ps5 = ExitStack()
        ps_s0 = ps5.enter_context(nc.psum_tensor("p5s0", [128, 512], F32))
        ps_s1 = ps5.enter_context(nc.psum_tensor("p5s1", [128, 512], F32))
        ps_av = ps5.enter_context(nc.psum_tensor("p5av", [128, 512], F32))
        ps_l = ps5.enter_context(nc.psum_tensor("p5l", [1, 512], F32))
        ps_rlb = ps5.enter_context(nc.psum_tensor("p5rlb", [128, 512], F32))
        ps_r5 = ps5.enter_context(nc.psum_tensor("p5r5", [1, 512], F32))
        ps_cb = ps5.enter_context(nc.psum_tensor("p5cb", [128, 4], F32))

        @blk.scalar
        def _(s):
            bar(s, p5)
            for h in range(NHEADS):
                s_gd.wait(s, p5_gd_q(h))
                s_sd.wait(s, p5_sd_k(h))
                if h >= 1:
                    s_mm.wait(s, p5_mm_kn(h - 1, 7))
                ins = s.activation(sq_q[:],
                                   qh[:, (h % 2) * 2048:(h % 2) * 2048 + 2048],
                                   AF.Square, scale=float(np.sqrt(CB_Q)))
                s_se.inc(ins, 1)
                ins = s.activation(sq_k[:],
                                   kh[:, (h % 2) * 4096:(h % 2) * 4096 + 4096],
                                   AF.Square, scale=float(np.sqrt(CB_K)))
                s_se.inc(ins, 1)
                for qb in range(4):
                    for kc in range(32):
                        s_mm.wait(s, p5_mm_s(h, qb, kc))
                        if qb == 0 and kc == 0:
                            s_ve.wait(s, p5_ve_ccols(h))
                        ins = s.activation(
                            pt_s[:, (kc % 4) * 512:(kc % 4) * 512 + 512],
                            (ps_s0 if kc % 2 == 0 else ps_s1)[:],
                            AF.Exp, bias=ccols[:, qb:qb + 1])
                        s_se.inc(ins, 1)

        assert s_se.n == p5[3] + P5_SE_H * NHEADS

        @blk.tensor
        def _(t):
            bar(t, p5)
            for h in range(NHEADS):
                s_se.wait(t, p5_se_sqq(h))
                for c in range(4):
                    if h + c > 0:
                        s_ve.wait(t, p5_ve_qn(h, c - 1) if c > 0 else
                                  p5_ve_kn(h - 1, 7))
                    s_mm.inc(t.matmul(ps_r5[:], onescol,
                                      sq_q[:, c * 512:(c + 1) * 512],
                                      start=True, stop=True), 1)
                s_se.wait(t, p5_se_sqk(h))
                for c in range(8):
                    s_ve.wait(t, p5_ve_kn(h, c - 1) if c > 0 else
                              p5_ve_qn(h, 3))
                    s_mm.inc(t.matmul(ps_r5[:], onescol,
                                      sq_k[:, c * 512:(c + 1) * 512],
                                      start=True, stop=True), 1)
                s_ve.wait(t, p5_ve_cmath(h))
                if h >= 1:
                    s_ve.wait(t, p5_ve_ccols(h - 1))
                s_mm.inc(t.matmul(ps_cb[:], c_onesf[0:1, :],
                                  crow[0:1, 4:8], start=True, stop=True), 1)
                for qb in range(4):
                    Q = 4 * h + qb
                    for kc in range(2):
                        if Q >= 1:
                            s_se.wait(t, p5_se_exp(h, qb - 1, 30 + kc)
                                      if qb >= 1 else
                                      p5_se_exp(h - 1, 3, 30 + kc))
                        ins = t.matmul(
                            (ps_s0 if kc % 2 == 0 else ps_s1)[:],
                            kh[:, (h % 2) * 4096 + kc * 128:
                               (h % 2) * 4096 + kc * 128 + 128],
                            qh[:, (h % 2) * 2048 + qb * 512:
                               (h % 2) * 2048 + qb * 512 + 512],
                            start=True, stop=True)
                        s_mm.inc(ins, 1)
                    for kc in range(32):
                        s_se.wait(t, p5_se_exp(h, qb, kc))
                        if kc == 0 and Q >= 1:
                            s_ve.wait(t, p5_ve_att(h, qb - 1) if qb >= 1
                                      else p5_ve_att(h - 1, 3))
                        t.matmul(ps_av[:],
                                 vv_s[:, kc * 512 + (h % 4) * 128:
                                      kc * 512 + (h % 4) * 128 + 128],
                                 pt_s[:, (kc % 4) * 512:
                                      (kc % 4) * 512 + 512],
                                 start=(kc == 0), stop=(kc == 31))
                        ins = t.matmul(ps_l[:], onescol,
                                       pt_s[:, (kc % 4) * 512:
                                            (kc % 4) * 512 + 512],
                                       start=(kc == 0), stop=(kc == 31))
                        if kc == 31:
                            s_mm.inc(ins, 1)
                        if kc < 30:
                            ins = t.matmul(
                                (ps_s0 if kc % 2 == 0 else ps_s1)[:],
                                kh[:, (h % 2) * 4096 + (kc + 2) * 128:
                                   (h % 2) * 4096 + (kc + 2) * 128 + 128],
                                qh[:, (h % 2) * 2048 + qb * 512:
                                   (h % 2) * 2048 + qb * 512 + 512],
                                start=True, stop=True)
                            s_mm.inc(ins, 1)
                    s_ve.wait(t, p5_ve_rlbf(h, qb))
                    s_mm.inc(t.matmul(ps_rlb[:], c_onesf[0:1, :],
                                      rl_row[0:1, 0:512],
                                      start=True, stop=True), 1)
            s_ve.wait(t, p5[4] + P5_VE_H * NHEADS)
            s_se.wait(t, p5[3] + P5_SE_H * NHEADS)

        assert s_mm.n == p5[2] + P5_MM_H * NHEADS, (s_mm.n,)

        @blk.vector
        def _(v):
            bar(v, p5)
            for h in range(NHEADS):
                for c in range(4):
                    s_mm.wait(v, p5_mm_qn(h, c))
                    ins = v.tensor_copy(qn2row[0:1, c * 512:(c + 1) * 512],
                                        ps_r5[:])
                    s_ve.inc(ins, 1)
                for c in range(8):
                    s_mm.wait(v, p5_mm_kn(h, c))
                    ins = v.tensor_copy(kn2row[0:1, c * 512:(c + 1) * 512],
                                        ps_r5[:])
                    s_ve.inc(ins, 1)
                v.tensor_reduce(crow[0:1, 0:4],
                                qn2row.rearrange("p (qb c) -> p qb c", qb=4),
                                AX.X, ALU.max)
                v.tensor_reduce(rowtmp[0:1, 0:1], kn2row[0:1, :], AX.X,
                                ALU.max)
                v.tensor_single_scalar(crow[0:1, 4:8], crow[0:1, 0:4],
                                       rowtmp[0:1, 0:1], ALU.add)
                ins = v.tensor_scalar_mul(crow[0:1, 4:8], crow[0:1, 4:8],
                                          -1.0)
                s_ve.inc(ins, 4)
                s_mm.wait(v, p5_mm_cb(h))
                if h >= 1:
                    s_se.wait(v, p5[3] + P5_SE_H * h)
                s_ve.inc(v.tensor_copy(ccols[:], ps_cb[:]), 1)
                for qb in range(4):
                    Q = 4 * h + qb
                    s_mm.wait(v, p5_mm_avl(h, qb))
                    s_ve.inc(v.reciprocal(rl_row[0:1, :], ps_l[:]), 1)
                    ins = v.tensor_copy(rowtmp2[0:1, :], rl_row[0:1, :])
                    s_ve.inc(ins, 1)
                    s_mm.wait(v, p5_mm_rlb(h, qb))
                    s_ve.inc(v.tensor_copy(rlb_sb[:], ps_rlb[:]), 1)
                    if Q >= 2:
                        s_sd.wait(v, p5_sd_st((Q - 2) // 4, (Q - 2) % 4))
                    ins = v.tensor_mul(att_t[:, (Q % 2) * 512:
                                              (Q % 2) * 512 + 512],
                                       ps_av[:], rlb_sb[:])
                    s_ve.inc(ins, 1)

        assert s_ve.n == p5[4] + P5_VE_H * NHEADS
        ps5.close()
        p5scope.close()
        p45scope.close()
        streamscope.close()

        if d_att is not None and upto == "p5":
            pda = snap()
            da_es = ExitStack()
            dab = sb(da_es, "dab", [128, 512], BF16)

            @blk.sync
            def _(sy):
                bar(sy, pda)
                for h in range(NHEADS):
                    for qb in range(4):
                        s_sd.inc(sy.dma_start(
                            out=dab[:],
                            in_=s_athbm[h * 128:(h + 1) * 128,
                                        qb * 512:(qb + 1) * 512]), 16)
                        sy.wait_ge(s_sd.h, s_sd.n)
                        s_sd.inc(sy.dma_start(
                            out=d_att[:, (4 * h + qb) * 512:
                                      (4 * h + qb) * 512 + 512],
                            in_=dab[:]), 16)
                        sy.wait_ge(s_sd.h, s_sd.n)
            da_es.close()

    if upto not in ("p1", "p2", "p3", "p4", "p5"):
        p5escope = ExitStack()
        wo_s = sb(p5escope, "wo_s", [128, NDC * 1024], BF16)
        att_in = sb(p5escope, "att_in", [128, 2 * NDC * 512], BF16)
        x2o_in = sb(p5escope, "x2o_in", [128, NDC * TOWN], BF16)
        normed = sb(p5escope, "normed", [128, 2 * NDC * 512], BF16)
        x3_t = sb(p5escope, "x3_t", [128, 2 * NDC * 512], BF16)
        t5_sb = sb(p5escope, "t5_sb", [128, 512], F32)
        c0_sb = sb(p5escope, "c0_sb", [128, NDC], F32)
        c_col = sb(p5escope, "c_col", [128, NDC], F32)

        def attin_ap(i, dc):
            s0 = ((i % 2) * NDC + dc) * 512
            return att_in[:, s0:s0 + 512]

        pw = snap()

        @blk.gpsimd
        def _(g):
            bar(g, pw)
            for rc in range(NDC):
                s_gd.inc(g.dma_start(
                    out=wo_s[:, rc * 1024:(rc + 1) * 1024],
                    in_=iw["w1o"][rc * 128:(rc + 1) * 128, :]), 16)

        psc = ExitStack()
        ps_c = psc.enter_context(nc.psum_tensor("p5c", [128, NDC], F32))

        @blk.tensor
        def _(t):
            bar(t, pw)
            t.wait_ge(s_gd.h, pw[0] + 16 * NDC)
            for dc in range(NDC):
                for rc in range(NDC):
                    ins = t.matmul(ps_c[:, dc:dc + 1],
                                   wo_s[:, rc * 1024 + dc * 128:
                                        rc * 1024 + dc * 128 + 128],
                                   v_lnb[:, rc:rc + 1],
                                   start=(rc == 0), stop=(rc == NDC - 1))
            s_mm.inc(ins, 1)

        pw_mm = s_mm.n

        @blk.scalar
        def _(s):
            bar(s, pw)
            s.wait_ge(s_mm.h, pw_mm)
            s_se.inc(s.activation(c0_sb[:], ps_c[:], AF.Copy, scale=0.2), 1)

        pw_se = s_se.n

        @blk.vector
        def _(v):
            bar(v, pw)
            v.wait_ge(s_se.h, pw_se)
            s_ve.inc(v.tensor_add(c_col[:], c0_sb[:], vsb["bo_c"][:]), 1)
            for rc in range(NDC):
                ins = v.tensor_scalar_mul(wo_s[:, rc * 1024:(rc + 1) * 1024],
                                          wo_s[:, rc * 1024:(rc + 1) * 1024],
                                          v_lnw02[:, rc:rc + 1])
                s_ve.inc(ins, 1)

        psc.close()

        def att_loader(g, i):
            for dc in range(NDC):
                s_gd.inc(g.dma_start(
                    out=attin_ap(i, dc),
                    in_=s_athbm[dc * 128:(dc + 1) * 128,
                                i * 512:(i + 1) * 512]), 16)

        emit_ln_stats("p5e", attin_ap, NBO, 0, xloader=att_loader, nload=NDC)

        p5e = snap()
        P5E_MM = 10
        P5E_VE = 34
        P5E_SD = 16 * 8

        def p5e_mm_tA(i):
            return p5e[2] + P5E_MM * i + 1

        def p5e_mm_tB(i):
            return p5e[2] + P5E_MM * i + 2

        def p5e_mm_g(i, dc):
            return p5e[2] + P5E_MM * i + 2 + (dc + 1)

        def p5e_ve(i, pos):
            return p5e[4] + P5E_VE * i + pos

        def p5e_gd(i):
            return p5e[0] + 16 * NDC + 16 * NDC * (i + 1)

        def p5e_sd(i, dc):
            return p5e[1] + P5E_SD * i + 16 * (dc + 1)

        @blk.gpsimd
        def _(g):
            bar(g, p5e)
            for dc in range(NDC):
                s_gd.inc(g.dma_start(
                    out=x2o_in[:, dc * TOWN:(dc + 1) * TOWN],
                    in_=s_x2hbm[dc * 128:(dc + 1) * 128,
                                bass.ts(hval["off"], TOWN)]), 16)
            for i in range(NBO):
                if i >= 2:
                    s_ve.wait(g, p5e_ve(i - 1, 34))
                for dc in range(NDC):
                    s_gd.inc(g.dma_start(
                        out=attin_ap(i, dc),
                        in_=s_athbm[dc * 128:(dc + 1) * 128,
                                    i * 512:(i + 1) * 512]), 16)

        pse = ExitStack()
        ps_b5 = pse.enter_context(nc.psum_tensor("p5eb", [128, 512], F32))
        ps_o5 = pse.enter_context(nc.psum_tensor("p5eo", [128, 1024], F32))

        @blk.tensor
        def _(t):
            bar(t, p5e)
            for i in range(NBO):
                if i >= 1:
                    s_ve.wait(t, p5e_ve(i - 1, 2))
                s_mm.inc(t.matmul(ps_b5[:], onesrow,
                                  mb_rows[0:1, i * 512:i * 512 + 512],
                                  start=True, stop=True), 1)
                s_ve.wait(t, p5e_ve(i, 1))
                s_mm.inc(t.matmul(ps_b5[:], onesrow,
                                  rb_rows[0:1, i * 512:i * 512 + 512],
                                  start=True, stop=True), 1)
                s_ve.wait(t, p5e_ve(i, 18))
                for dc in range(NDC):
                    if dc >= 2:
                        s_ve.wait(t, p5e_ve(i, 18 + 2 * (dc - 2) + 2))
                    elif i >= 1:
                        s_ve.wait(t, p5e_ve(i - 1, 18 + 2 * (6 + dc) + 2))
                    sl = (dc % 2) * 512
                    for ec in range(NDC):
                        s0 = ((i % 2) * NDC + ec) * 512
                        ins = t.matmul(ps_o5[:, sl:sl + 512],
                                       wo_s[:, ec * 1024 + dc * 128:
                                            ec * 1024 + dc * 128 + 128],
                                       normed[:, s0:s0 + 512],
                                       start=(ec == 0), stop=(ec == NDC - 1))
                    s_mm.inc(ins, 1)
            s_ve.wait(t, p5e[4] + P5E_VE * NBO)

        assert s_mm.n == p5e[2] + P5E_MM * NBO

        @blk.vector
        def _(v):
            bar(v, p5e)
            for i in range(NBO):
                s_mm.wait(v, p5e_mm_tA(i))
                s_ve.inc(v.tensor_copy(mbt[:], ps_b5[:]), 1)
                s_mm.wait(v, p5e_mm_tB(i))
                s_ve.inc(v.tensor_copy(rbt[:], ps_b5[:]), 1)
                s_gd.wait(v, p5e_gd(i))
                for dc in range(NDC):
                    s0 = ((i % 2) * NDC + dc) * 512
                    v.tensor_sub(normed[:, s0:s0 + 512], attin_ap(i, dc),
                                 mbt[:])
                    ins = v.tensor_mul(normed[:, s0:s0 + 512],
                                       normed[:, s0:s0 + 512], rbt[:])
                    s_ve.inc(ins, 2)
                for dc in range(NDC):
                    s_mm.wait(v, p5e_mm_g(i, dc))
                    if i >= 2:
                        s_sd.wait(v, p5e_sd(i - 2, dc))
                    sl = (dc % 2) * 512
                    s0 = ((i % 2) * NDC + dc) * 512
                    v.tensor_add(t5_sb[:], ps_o5[:, sl:sl + 512],
                                 x2o_in[:, dc * TOWN + i * 512:
                                        dc * TOWN + i * 512 + 512])
                    ins = v.tensor_scalar_add(x3_t[:, s0:s0 + 512],
                                              t5_sb[:], c_col[:, dc:dc + 1])
                    s_ve.inc(ins, 2)

        assert s_ve.n == p5e[4] + P5E_VE * NBO

        @blk.sync
        def _(sy):
            bar(sy, p5e)
            for i in range(NBO):
                for dc in range(NDC):
                    s_ve.wait(sy, p5e_ve(i, 18 + 2 * (dc + 1)))
                    s0 = ((i % 2) * NDC + dc) * 512
                    s_sd.inc(sy.dma_start(
                        out=s_x3hbm[dc * 128:(dc + 1) * 128,
                                    i * 512:(i + 1) * 512],
                        in_=x3_t[:, s0:s0 + 512]), 16)

        assert s_sd.n == p5e[1] + P5E_SD * NBO
        pse.close()
        p5escope.close()

        if d_x3 is not None and upto in ("p5e", "p6"):
            pdx = snap()
            dx_es = ExitStack()
            dxb = sb(dx_es, "dxb", [128, 512], BF16)

            @blk.sync
            def _(sy):
                bar(sy, pdx)
                for i in range(NBO):
                    for dc in range(NDC):
                        s_sd.inc(sy.dma_start(
                            out=dxb[:],
                            in_=s_x3hbm[dc * 128:(dc + 1) * 128,
                                        i * 512:(i + 1) * 512]), 16)
                        sy.wait_ge(s_sd.h, s_sd.n)
                        s_sd.inc(sy.dma_start(
                            out=d_x3[:, (i * NDC + dc) * 512:
                                     (i * NDC + dc) * 512 + 512],
                            in_=dxb[:]), 16)
                        sy.wait_ge(s_sd.h, s_sd.n)
            dx_es.close()

    if upto in ("p6a", "p6b", "p6"):
        p6acc = ExitStack()
        out_acc = sb(p6acc, "out_acc", [128, NDC * TOWN], BF16)
        p6hi = ExitStack()
        xln4hi = sb(p6hi, "xln4hi", [128, NDC * TOWN], BF16)

        p6a = ExitStack()
        x3_in = sb(p6a, "x3_in", [128, 2 * NDC * 512], BF16)
        xf_f32 = sb(p6a, "xf_f32", [128, NDC * 512], F32)
        u1_f32 = sb(p6a, "u1_f32", [128, NDC * 512], F32)
        mid_b = sb(p6a, "mid_b", [128, NDC * 512], BF16)
        lo_b = sb(p6a, "lo_b", [128, NDC * 512], BF16)
        sc_all = sb(p6a, "sc_all", [128, 16 * NE], F32)
        cnt_t = sb(p6a, "cnt_t", [128, NE], F32)
        cnt_all = sb(p6a, "cnt_all", [128, 16 * NE], F32)
        mask_t = sb(p6a, "mask_t", [128, NE], F32)
        w_all = sb(p6a, "w_all", [128, 16 * NE], F32)
        wrT_sb = sb(p6a, "wrT_sb", [8, 16 * 128], F32)

        def x3in_ap(i, dc):
            s0 = ((i % 2) * NDC + dc) * 512
            return x3_in[:, s0:s0 + 512]

        def x3_loader(g, i):
            for dc in range(NDC):
                s_gd.inc(g.dma_start(
                    out=x3in_ap(i, dc),
                    in_=s_x3hbm[dc * 128:(dc + 1) * 128,
                                i * 512:(i + 1) * 512]), 16)

        emit_ln_stats("p6s", x3in_ap, NBO, 0, xloader=x3_loader, nload=NDC)

        p6 = snap()
        P6A_MM = 10
        P6A_VE = 90
        P6A_SE = 4
        P6A_GD = 16 * NDC

        def p6a_mm_tA(i):
            return p6[2] + P6A_MM * i + 1

        def p6a_mm_tB(i):
            return p6[2] + P6A_MM * i + 2

        def p6a_mm_sc(i, tt):
            return p6[2] + P6A_MM * i + 2 + tt + 1

        def p6a_mm_tr(i, tt):
            return p6[2] + P6A_MM * i + 6 + tt + 1

        def p6a_ve(i, pos):
            return p6[4] + P6A_VE * i + pos

        def p6a_se(i, tt):
            return p6[3] + P6A_SE * i + tt + 1

        def p6a_gd(i):
            return p6[0] + P6A_GD * (i + 1)

        @blk.gpsimd
        def _(g):
            bar(g, p6)
            for i in range(NBO):
                if i >= 2:
                    s_ve.wait(g, p6a_ve(i - 1, 18))
                x3_loader(g, i)

        ps6a = ExitStack()
        ps_b6 = ps6a.enter_context(nc.psum_tensor("p6b_", [128, 512], F32))
        ps_sc = ps6a.enter_context(nc.psum_tensor("p6sc", [128, NE], F32))
        ps_w = ps6a.enter_context(nc.psum_tensor("p6w", [8, 128], F32))

        @blk.scalar
        def _(s):
            bar(s, p6)
            for i in range(NBO):
                for tt in range(4):
                    gt = 4 * i + tt
                    s_mm.wait(s, p6a_mm_sc(i, tt))
                    ins = s.activation(sc_all[:, gt * NE:(gt + 1) * NE],
                                       ps_sc[:], AF.Copy)
                    s_se.inc(ins, 1)

        @blk.tensor
        def _(t):
            bar(t, p6)
            for i in range(NBO):
                if i >= 1:
                    s_ve.wait(t, p6a_ve(i - 1, 2))
                s_mm.inc(t.matmul(ps_b6[:], onesrow,
                                  mb_rows[0:1, i * 512:i * 512 + 512],
                                  start=True, stop=True), 1)
                s_ve.wait(t, p6a_ve(i, 1))
                s_mm.inc(t.matmul(ps_b6[:], onesrow,
                                  rb_rows[0:1, i * 512:i * 512 + 512],
                                  start=True, stop=True), 1)
                for tt in range(4):
                    if tt == 0:
                        s_ve.wait(t, p6a_ve(i, 50))
                    else:
                        s_se.wait(t, p6a_se(i, tt - 1))
                    combos = [("hi", wg_hi), ("hi", wg_mid), ("mid", wg_hi),
                              ("hi", wg_lo), ("lo", wg_hi), ("mid", wg_mid)]
                    nmm = 0
                    for xs_, wg_x in combos:
                        for dc in range(NDC):
                            if xs_ == "hi":
                                lx = xln4hi[:, dc * TOWN + i * 512 + tt * 128:
                                            dc * TOWN + i * 512 + tt * 128
                                            + 128]
                            elif xs_ == "mid":
                                lx = mid_b[:, dc * 512 + tt * 128:
                                           dc * 512 + tt * 128 + 128]
                            else:
                                lx = lo_b[:, dc * 512 + tt * 128:
                                          dc * 512 + tt * 128 + 128]
                            nmm += 1
                            ins = t.matmul(ps_sc[:], lx,
                                           wg_x[:, dc * NE:(dc + 1) * NE],
                                           start=(nmm == 1), stop=(nmm == 48))
                    s_mm.inc(ins, 1)
                for tt in range(4):
                    gt = 4 * i + tt
                    s_ve.wait(t, p6a_ve(i, 82 + tt + 1))
                    if i + tt > 0:
                        pi, ptt = (i, tt - 1) if tt >= 1 else (i - 1, 3)
                        s_ve.wait(t, p6a_ve(pi, 86 + ptt + 1))
                    s_mm.inc(t.transpose(ps_w[:],
                                         w_all[:, gt * NE:(gt + 1) * NE],
                                         c_id[:]), 1)
            s_ve.wait(t, p6[4] + P6A_VE * NBO)

        assert s_mm.n == p6[2] + P6A_MM * NBO

        @blk.vector
        def _(v):
            bar(v, p6)
            for i in range(NBO):
                s_mm.wait(v, p6a_mm_tA(i))
                s_ve.inc(v.tensor_copy(mbt[:], ps_b6[:]), 1)
                s_mm.wait(v, p6a_mm_tB(i))
                s_ve.inc(v.tensor_copy(rbt[:], ps_b6[:]), 1)
                s_gd.wait(v, p6a_gd(i))
                for dc in range(NDC):
                    v.tensor_sub(xf_f32[:, dc * 512:(dc + 1) * 512],
                                 x3in_ap(i, dc), mbt[:])
                    ins = v.tensor_mul(xf_f32[:, dc * 512:(dc + 1) * 512],
                                       xf_f32[:, dc * 512:(dc + 1) * 512],
                                       rbt[:])
                    s_ve.inc(ins, 2)
                for dc in range(NDC):
                    ins = v.tensor_copy(
                        xln4hi[:, dc * TOWN + i * 512:
                               dc * TOWN + i * 512 + 512],
                        xf_f32[:, dc * 512:(dc + 1) * 512])
                    s_ve.inc(ins, 1)
                for dc in range(NDC):
                    ins = v.tensor_sub(u1_f32[:, dc * 512:(dc + 1) * 512],
                                       xf_f32[:, dc * 512:(dc + 1) * 512],
                                       xln4hi[:, dc * TOWN + i * 512:
                                              dc * TOWN + i * 512 + 512])
                    s_ve.inc(ins, 1)
                for dc in range(NDC):
                    ins = v.tensor_copy(mid_b[:, dc * 512:(dc + 1) * 512],
                                        u1_f32[:, dc * 512:(dc + 1) * 512])
                    s_ve.inc(ins, 1)
                for dc in range(NDC):
                    ins = v.tensor_sub(lo_b[:, dc * 512:(dc + 1) * 512],
                                       u1_f32[:, dc * 512:(dc + 1) * 512],
                                       mid_b[:, dc * 512:(dc + 1) * 512])
                    s_ve.inc(ins, 1)
                for tt in range(4):
                    gt = 4 * i + tt
                    sc_t = sc_all[:, gt * NE:(gt + 1) * NE]
                    s_se.wait(v, p6a_se(i, tt))
                    v.tensor_single_scalar(cnt_t[:], sc_t, sc_t[:, 0:1],
                                           ALU.is_lt)
                    cb = [cnt_t, cnt_all[:, gt * NE:(gt + 1) * NE]]
                    for e in range(1, NE):
                        ins = v.scalar_tensor_tensor(cb[e % 2][:], sc_t,
                                                     sc_t[:, e:e + 1],
                                                     cb[(e + 1) % 2][:],
                                                     ALU.is_lt, ALU.add)
                    s_ve.inc(ins, 8)
                for tt in range(4):
                    gt = 4 * i + tt
                    ins = v.scalar_tensor_tensor(
                        w_all[:, gt * NE:(gt + 1) * NE],
                        cnt_all[:, gt * NE:(gt + 1) * NE], 1.5,
                        sc_all[:, gt * NE:(gt + 1) * NE],
                        ALU.is_le, ALU.mult)
                    s_ve.inc(ins, 1)
                for tt in range(4):
                    gt = 4 * i + tt
                    s_mm.wait(v, p6a_mm_tr(i, tt))
                    ins = v.tensor_copy(
                        wrT_sb[:, gt * 128:gt * 128 + 128], ps_w[:])
                    s_ve.inc(ins, 1)

        assert s_ve.n == p6[4] + P6A_VE * NBO

        p6a_end = snap()

        @blk.sync
        def _(sy):
            bar(sy, p6a_end)
            s_sd.inc(sy.dma_start(out=s_wrow[:, :], in_=wrT_sb[:, :]), 16)

        if d_w is not None and upto in ("p6a", "p6b", "p6"):
            pdw = snap()

            @blk.sync
            def _(sy):
                bar(sy, pdw)
                s_sd.inc(sy.dma_start(out=d_w[:, :], in_=wrT_sb[:, :]), 16)
        if d_sc is not None and upto in ("p6a", "p6b", "p6"):
            pdsc = snap()

            @blk.sync
            def _(sy):
                bar(sy, pdsc)
                s_sd.inc(sy.dma_start(out=d_sc[:, :], in_=sc_all[:, :]), 16)
        if d_cnt is not None and upto in ("p6a", "p6b", "p6"):
            pdcnt = snap()

            @blk.sync
            def _(sy):
                bar(sy, pdcnt)
                s_sd.inc(sy.dma_start(out=d_cnt[:, :], in_=w_all[:, :]), 16)

        ps6a.close()
        p6a.close()

    if upto in ("p6b", "p6"):
        p6b = ExitStack()
        w3_s = sb(p6b, "w3_s", [128, NHC * 1024], BF16)
        prod_sb = sb(p6b, "prod_sb", [128, NHC * 512], BF16)
        w1sl = sb(p6b, "w1sl", [128, 2 * 1024], BF16)
        w2sl = sb(p6b, "w2sl", [128, 2 * 1024], BF16)
        g_sb = sb(p6b, "g_sb", [128, 2 * 512], BF16)
        wbc_sb = sb(p6b, "wbc_sb", [128, 4 * 512], BF16)
        w_row = sb(p6b, "w_row", [1, TOWN], BF16)
        t6_sb = sb(p6b, "t6_sb", [128, 2 * 512], F32)

        pb = snap()
        E_GD = 16 * (1 + 22 + 4 * 44)
        E_MM = 4 + 4 * 52
        E_SE = 4 * 22

        def p6b_ve_base(e):
            if e < 0:
                return 0
            return pb[4] + 124 * min(e, 1) + 156 * max(e - 1, 0)

        def p6b_gd_wrow(e):
            return pb[0] + E_GD * e + 16 * 1

        def p6b_gd_w3(e):
            return pb[0] + E_GD * e + 16 * 23

        def p6b_gd_w12(e, tb, hc):
            return pb[0] + E_GD * e + 16 * (23 + tb * 44 + 2 * hc + 2)

        def p6b_mm_wb(e, k):
            return pb[2] + E_MM * e + k + 1

        def p6b_mm_gu(e, tb, hc, j):
            return pb[2] + E_MM * e + 4 + tb * 52 + hc * 2 + j + 1

        def p6b_mm_do(e, tb, dc):
            return pb[2] + E_MM * e + 4 + tb * 52 + 44 + dc + 1

        def p6b_se(e, tb, hc):
            return pb[3] + E_SE * e + tb * 22 + hc + 1

        def p6b_ve_wbc(e, k):
            return p6b_ve_base(e) + k + 1

        def p6b_ve_prod(e, tb, hc):
            npt = 8 if e == 0 else 16
            return p6b_ve_base(e) + 4 + tb * (22 + npt) + hc + 1

        def p6b_ve_acc(e, tb, dc):
            npt = 8 if e == 0 else 16
            na = 1 if e == 0 else 2
            return p6b_ve_base(e) + 4 + tb * (22 + npt) + 22 + \
                na * dc + na

        @blk.gpsimd
        def _(g):
            bar(g, pb)
            for e in range(NEB):
                s_gd.inc(g.dma_start(out=w_row[:],
                                     in_=s_wrow[e:e + 1, :]), 16)
                for hc in range(NHC):
                    if e >= 1 and hc == 0:
                        s_mm.wait(g, pb[2] + E_MM * e)
                    s_gd.inc(g.dma_start(
                        out=w3_s[:, hc * 1024:(hc + 1) * 1024],
                        in_=i_w3t[e, hc]), 16)
                for tb in range(4):
                    for hc in range(NHC):
                        n = tb * 22 + hc
                        if e + n > 0:
                            pe, pn = (e, n - 2) if n >= 2 else \
                                (e - 1, 86 + n)
                            s_mm.wait(g, p6b_mm_gu(pe, pn // 22, pn % 22, 1))
                        sl = (n % 2) * 1024
                        s_gd.inc(g.dma_start(out=w1sl[:, sl:sl + 1024],
                                             in_=i_w1t[e, hc]), 16)
                        s_gd.inc(g.dma_start(out=w2sl[:, sl:sl + 1024],
                                             in_=i_w2t[e, hc]), 16)

        ps6b = ExitStack()
        ps_wb = ps6b.enter_context(nc.psum_tensor("p6wb", [128, 512], F32))
        ps_g = ps6b.enter_context(nc.psum_tensor("p6g", [128, 1024], F32))
        ps_u = ps6b.enter_context(nc.psum_tensor("p6u", [128, 1024], F32))
        ps_do = ps6b.enter_context(nc.psum_tensor("p6do", [128, 1024], F32))

        @blk.tensor
        def _(t):
            bar(t, pb)
            for e in range(NEB):
                t.wait_ge(s_gd.h, p6b_gd_wrow(e))
                for k in range(4):
                    if e + k > 0:
                        pe, pk = (e, k - 1) if k >= 1 else (e - 1, 3)
                        s_ve.wait(t, p6b_ve_wbc(pe, pk))
                    ins = t.matmul(ps_wb[:], onesrow,
                                   w_row[0:1, k * 512:(k + 1) * 512],
                                   start=True, stop=True)
                    s_mm.inc(ins, 1)
                for tb in range(4):
                    for hc in range(NHC):
                        n = tb * 22 + hc
                        t.wait_ge(s_gd.h, p6b_gd_w12(e, tb, hc))
                        sl = (n % 2) * 1024
                        for j, (wsl, ps) in enumerate(
                                [(w1sl, ps_g), (w2sl, ps_u)]):
                            gn = n * 2 + j
                            if e + gn > 0:
                                pe, pg = (e, gn - 4) if gn >= 4 else \
                                    (e - 1, 172 + gn)
                                pn2, pj = divmod(pg, 2)
                                if pj == 0:
                                    s_se.wait(t, p6b_se(
                                        pe, pn2 // 22, pn2 % 22))
                                else:
                                    s_ve.wait(t, p6b_ve_prod(
                                        pe, pn2 // 22, pn2 % 22))
                            psl = (n % 2) * 512
                            for dc in range(NDC):
                                ins = t.matmul(
                                    ps[:, psl:psl + 512],
                                    wsl[:, sl + dc * 128:
                                        sl + dc * 128 + 128],
                                    xln4hi[:, dc * TOWN + tb * 512:
                                           dc * TOWN + tb * 512 + 512],
                                    start=(dc == 0),
                                    stop=(dc == NDC - 1))
                            s_mm.inc(ins, 1)
                    t.wait_ge(s_gd.h, p6b_gd_w3(e))
                    s_ve.wait(t, p6b_ve_prod(e, tb, NHC - 1))
                    for dc in range(NDC):
                        if e * 32 + tb * 8 + dc >= 2:
                            pe, pn = (e, tb * 8 + dc - 2) \
                                if tb * 8 + dc >= 2 else (e - 1, 30 + tb * 8 + dc)
                            s_ve.wait(t, p6b_ve_acc(pe, pn // 8, pn % 8))
                        psl = (dc % 2) * 512
                        for hc in range(NHC):
                            ins = t.matmul(
                                ps_do[:, psl:psl + 512],
                                w3_s[:, hc * 1024 + dc * 128:
                                     hc * 1024 + dc * 128 + 128],
                                prod_sb[:, hc * 512:hc * 512 + 512],
                                start=(hc == 0), stop=(hc == NHC - 1))
                        s_mm.inc(ins, 1)

        assert s_mm.n == pb[2] + E_MM * NEB, (s_mm.n,)

        @blk.scalar
        def _(s):
            bar(s, pb)
            for e in range(NEB):
                for tb in range(4):
                    for hc in range(NHC):
                        n = tb * 22 + hc
                        s_mm.wait(s, p6b_mm_gu(e, tb, hc, 0))
                        if e * 88 + n >= 2:
                            pe, pn = (e, n - 2) if n >= 2 else \
                                (e - 1, 86 + n)
                            s_ve.wait(s, p6b_ve_prod(pe, pn // 22, pn % 22))
                        ins = s.activation(
                            g_sb[:, (n % 2) * 512:(n % 2) * 512 + 512],
                            ps_g[:, (n % 2) * 512:(n % 2) * 512 + 512],
                            AF.Silu)
                        s_se.inc(ins, 1)

        assert s_se.n == pb[3] + E_SE * NEB

        @blk.vector
        def _(v):
            bar(v, pb)
            for e in range(NEB):
                for k in range(4):
                    s_mm.wait(v, p6b_mm_wb(e, k))
                    ins = v.tensor_copy(wbc_sb[:, k * 512:(k + 1) * 512],
                                        ps_wb[:])
                    s_ve.inc(ins, 1)
                for tb in range(4):
                    for hc in range(NHC):
                        n = tb * 22 + hc
                        s_mm.wait(v, p6b_mm_gu(e, tb, hc, 1))
                        s_se.wait(v, p6b_se(e, tb, hc))
                        ins = v.tensor_mul(
                            prod_sb[:, hc * 512:hc * 512 + 512],
                            ps_u[:, (n % 2) * 512:(n % 2) * 512 + 512],
                            g_sb[:, (n % 2) * 512:(n % 2) * 512 + 512])
                        s_ve.inc(ins, 1)
                    for dc in range(NDC):
                        s_mm.wait(v, p6b_mm_do(e, tb, dc))
                        psl = (dc % 2) * 512
                        oap = out_acc[:, dc * TOWN + tb * 512:
                                      dc * TOWN + tb * 512 + 512]
                        if e == 0:
                            ins = v.tensor_mul(oap, ps_do[:, psl:psl + 512],
                                               wbc_sb[:, tb * 512:
                                                      (tb + 1) * 512])
                            s_ve.inc(ins, 1)
                        else:
                            v.tensor_mul(
                                t6_sb[:, (dc % 2) * 512:(dc % 2) * 512 + 512],
                                ps_do[:, psl:psl + 512],
                                wbc_sb[:, tb * 512:(tb + 1) * 512])
                            ins = v.tensor_add(
                                oap, oap,
                                t6_sb[:, (dc % 2) * 512:(dc % 2) * 512 + 512])
                            s_ve.inc(ins, 2)

        assert s_ve.n == p6b_ve_base(NEB), (s_ve.n,)
        ps6b.close()
        p6b.close()
        p6hi.close()

    if upto == "p6":
        p6c = ExitStack()
        x3f_sb = sb(p6c, "x3f_sb", [128, 2 * 512], BF16)
        out_sb = sb(p6c, "out_sb", [128, 2 * 512], F32)

        pc = snap()

        def p6c_gd_ld(n):
            return pc[0] + 16 * (n + 1)

        def p6c_ve(n):
            return pc[4] + n + 1

        def p6c_sd_st(n):
            return pc[1] + 16 * (n + 1)

        @blk.gpsimd
        def _(g):
            bar(g, pc)
            for n in range(32):
                tb, dc = n // 8, n % 8
                if n >= 2:
                    s_ve.wait(g, p6c_ve(n - 2))
                s_gd.inc(g.dma_start(
                    out=x3f_sb[:, (n % 2) * 512:(n % 2) * 512 + 512],
                    in_=s_x3hbm[dc * 128:(dc + 1) * 128,
                                tb * 512:(tb + 1) * 512]), 16)

        @blk.vector
        def _(v):
            bar(v, pc)
            for n in range(32):
                tb, dc = n // 8, n % 8
                s_gd.wait(v, p6c_gd_ld(n))
                if n >= 2:
                    s_sd.wait(v, p6c_sd_st(n - 2))
                ins = v.tensor_add(
                    out_sb[:, (n % 2) * 512:(n % 2) * 512 + 512],
                    out_acc[:, dc * TOWN + tb * 512:
                            dc * TOWN + tb * 512 + 512],
                    x3f_sb[:, (n % 2) * 512:(n % 2) * 512 + 512])
                s_ve.inc(ins, 1)

        @blk.sync
        def _(sy):
            bar(sy, pc)
            for n in range(32):
                tb, dc = n // 8, n % 8
                s_ve.wait(sy, p6c_ve(n))
                s_sd.inc(sy.dma_start(
                    out=o_out[dc * 128:(dc + 1) * 128,
                              tb * 512:(tb + 1) * 512],
                    in_=out_sb[:, (n % 2) * 512:(n % 2) * 512 + 512]), 16)

        p6c.close()
        p6acc.close()

    try:
        p6hi.close()
    except Exception:
        pass
    try:
        p6acc.close()
    except Exception:
        pass
    try:
        p45scope.close()
    except Exception:
        pass
    try:
        streamscope.close()
    except Exception:
        pass
    fin = snap()

    @blk.sync
    def _(sy):
        bar(sy, fin)
        sy.wait_ge(s_sd.h, s_sd.n)

    es.close()
    return nc, dbg_t



def _prep_inputs(inputs):
    f32 = np.float32
    A = np.ascontiguousarray
    X = np.asarray(inputs["L0_attn_X"], f32)

    def col(v):
        return A(np.asarray(v, f32).reshape(NDC, 128).T)

    base = {
        "w0q": A(np.asarray(inputs["L0_attn_Wq"], f32)),
        "w0k": A(np.asarray(inputs["L0_attn_Wk"], f32)),
        "w0v": A(np.asarray(inputs["L0_attn_Wv"], f32)),
        "w0c": A(np.asarray(inputs["L0_attn_Wc"], f32)),
        "w0fr": A(np.asarray(inputs["L0_ffn_W_r"], f32).T),
        "w0fk": A(np.asarray(inputs["L0_ffn_W_k"], f32).T),
        "w0fv": A(np.asarray(inputs["L0_ffn_W_v"], f32).T),
        "mur_c": col(inputs["L0_ffn_mu_r"]),
        "muk_c": col(inputs["L0_ffn_mu_k"]),
        "w1q": A(np.asarray(inputs["L1_attn_Wq"], f32)),
        "w1k": A(np.asarray(inputs["L1_attn_Wk"], f32)),
        "w1v": A(np.asarray(inputs["L1_attn_Wv"], f32)),
        "w1o": A(np.asarray(inputs["L1_attn_Wo"], f32)),
        "bq_c": col(inputs["L1_attn_bq"]),
        "bk_c": col(inputs["L1_attn_bk"]),
        "bo_c": col(inputs["L1_attn_bo"]),
        "lnw_c": col(inputs["L1_attn_ln_w"]),
        "lnb_c": col(inputs["L1_attn_ln_b"]),
        "bv_r": A(np.asarray(inputs["L1_attn_bv"], f32).reshape(1, D)),
        "lamv": A(np.concatenate([
            np.asarray(inputs["L1_attn_lambda_q1"], f32),
            np.asarray(inputs["L1_attn_lambda_k1"], f32),
            np.asarray(inputs["L1_attn_lambda_q2"], f32),
            np.asarray(inputs["L1_attn_lambda_k2"], f32)]).reshape(1, 512)),
        "wgT": A(np.asarray(inputs["L1_ffn_Wg"], f32).T),
        "w1t": A(np.asarray(inputs["L1_ffn_W1"], f32)
                 .reshape(NE, NHC, 128, NDC, 128).transpose(0, 1, 4, 3, 2)
                 .reshape(NE, NHC, 128, NDC * 128)),
        "w2t": A(np.asarray(inputs["L1_ffn_W2"], f32)
                 .reshape(NE, NHC, 128, NDC, 128).transpose(0, 1, 4, 3, 2)
                 .reshape(NE, NHC, 128, NDC * 128)),
        "w3t": A(np.asarray(inputs["L1_ffn_W3"], f32).transpose(0, 2, 1)
                 .reshape(NE, NHC, 128, D)),
        "c_tri": np.triu(np.ones((128, 128), f32)),
        "c_id": np.eye(128, dtype=f32),
        "c_ones": np.ones((128, 128), f32),
        "c_init": A(np.concatenate([
            np.full((128, 1), EPS_K, f32), np.zeros((128, 1), f32)], 1)),
    }
    in_maps = []
    for c in range(8):
        b_, h_ = c // 2, c % 2
        m = dict(base)
        m["x_in_T"] = A(X[b_].T)
        m["hflag"] = np.array([[h_]], dtype=np.uint32)
        in_maps.append(m)
    return in_maps


def kernel(**inputs):
    key = "full"
    if key not in _CACHE:
        _CACHE[key] = build_program(upto="p6")
    nc, _ = _CACHE[key]
    in_maps = _prep_inputs(inputs)
    res = run_bass_kernel_spmd(nc, in_maps, core_ids=list(range(8)))
    out = np.empty((4, T, D), np.float32)
    for c in range(8):
        b_, h_ = c // 2, c % 2
        out[b_, h_ * TOWN:(h_ + 1) * TOWN, :] = res.results[c]["out_T"].T
    return out
